# version 03
# baseline (speedup 1.0000x reference)
import numpy as np
import concourse.bass as bass
import concourse.mybir as mybir
from concourse.alu_op_type import AluOpType
from concourse.bass_utils import run_bass_kernel_spmd

F32 = mybir.dt.float32
BF16 = mybir.dt.bfloat16
AF = mybir.ActivationFunctionType
AX = mybir.AxisListType
OP = AluOpType

P = 128
T = 2048
D = 1024
DC = D // P
TC = 512
NTC = T // TC
DH = 64
NH = 16
DHID = 4096
NCC = DHID // P
STEPS = 6

ENGINES = ("tensor", "vector", "scalar", "sync", "gpsimd")

WTOT = 106400
W_X = 0
W_R1 = 32768
W_R2 = 49152
W_R3 = 65536
W_R4 = 81920
W_R5 = 98560
W_PERM = 105200
W_TMP = W_R5
W_STAT = W_R5 + 4096
W_SQ = W_R5 + 4288
W_FEAT = W_R5 + 5312
W_XTR = W_R5 + 5824

XA_WDK, XA_WDV, XA_WUK, XA_WUV = 0, 1024, 2048, 3072
XA_COS, XA_SIN, XA_PROT, XA_BUV = 4096, 6144, 8192, 8320
XA_CK, XA_CV = 9344, 11392
XA_QEXT = 13440
XA_KEXT = 17536
XA_EXPT = 17792
XA_REC = 21888
XA_RECB = 22912
XA_RECROW = 23936
XL_S0, XL_S1 = 0, 16384
XL_CTX, XL_REC, XL_RECB, XL_BV = 16384, 20544, 21568, 22592


class Sem:
    def __init__(self, nc, name):
        self.h = nc.alloc_semaphore(name=name)
        self.name = name
        self.n = 0

    def inc(self, k=1):
        self.n += k
        return self.n


class Rec:
    def __init__(self):
        self.ops = {e: [] for e in ENGINES}
        self.waited = {e: {} for e in ENGINES}
        self.phase = "init"
        self.ttags = []

    def E(self, eng, meth, *args, waits=(), inc=None, incn=1, **kw):
        wl = []
        wd = self.waited[eng]
        for (s, v) in waits:
            if v is None or v == 0:
                continue
            assert v <= s.n, f"[{eng}.{meth}] wait {s.name}>={v} but recorded {s.n}"
            if wd.get(s.name, -1) < v:
                wd[s.name] = v
                wl.append((s.h, v))
        val = inc.inc(incn) if inc is not None else None
        ih = inc.h if inc is not None else None
        tag = self.phase

        def f(h, meth=meth, args=args, kw=kw, wl=tuple(wl), ih=ih, incn=incn, tag=tag):
            for (sh, v) in wl:
                h.wait_ge(sh, v)
            r = getattr(h, meth)(*args, **kw)
            try:
                r.annotate(tag)
            except Exception:
                pass
            if ih is not None:
                r.then_inc(ih, incn)
        self.ops[eng].append(f)
        return val

    def wait(self, eng, *pairs):
        wd = self.waited[eng]
        for (s, v) in pairs:
            if v is None or v == 0:
                continue
            assert v <= s.n
            if wd.get(s.name, -1) < v:
                wd[s.name] = v

                def f(h, sh=s.h, v=v):
                    h.wait_ge(sh, v)
                self.ops[eng].append(f)


def build_nc(steps=STEPS):
    nc = bass.Bass("TRN2", target_bir_lowering=False)
    ap = {}

    def din(name, shape):
        ap[name] = nc.dram_tensor(name, list(shape), F32, kind="ExternalInput").ap()

    for nm, sh in [("x", (T, D)),
                   ("w1", (DHID, D)), ("w2", (DHID, D)), ("h1", (DHID, DHID)),
                   ("h2", (DHID, DHID)), ("w3", (D, DHID)),
                   ("s1c", (P, NCC)), ("s2c", (P, NCC)), ("sh1c", (P, NCC)),
                   ("sh2c", (P, NCC)), ("sw3c", (P, DC)),
                   ("wdk", (D, P)), ("wdv", (D, P)), ("wuk", (P, D)),
                   ("wuv", (P, D)), ("wo", (D, D)),
                   ("bdkc", (P, 1)), ("bdvc", (P, 1)), ("bukc", (P, DC)),
                   ("buv_row", (1, D)), ("boc", (P, DC)),
                   ("cosb", (P, T)), ("sinb", (P, T)), ("prot", (P, P)),
                   ("idf", (P, P)),
                   ("wq", (D, D)), ("wk", (D, D)), ("wv", (D, D)), ("wproj", (D, D)),
                   ("bqc", (P, DC)), ("bkc", (P, DC)), ("bv_row", (1, D)),
                   ("bprojc", (P, DC)), ("wqf", (DH, 256)), ("wkf", (DH, 256)),
                   ("l1w", (D, D)), ("l1a1", (D, D)), ("l1a2", (D, D)),
                   ("l1bc", (P, DC))]:
        din(nm, sh)
    out = nc.dram_tensor("out", [T, D], F32, kind="ExternalOutput").ap()

    w1bf = nc.dram_tensor("w1bf", [DHID, D], BF16, kind="Internal").ap()
    w2bf = nc.dram_tensor("w2bf", [DHID, D], BF16, kind="Internal").ap()
    h1bf = nc.dram_tensor("h1bf", [DHID, DHID], BF16, kind="Internal").ap()
    h2bf = nc.dram_tensor("h2bf", [DHID, DHID], BF16, kind="Internal").ap()
    w3bf = nc.dram_tensor("w3bf", [D, DHID], BF16, kind="Internal").ap()
    xsp = nc.dram_tensor("xsp", [P, DC * T], F32, kind="Internal").ap()

    with (
        nc.sbuf_tensor("SB", [P, WTOT], BF16) as SBt,
        nc.psum_tensor("psA", [P, 2048], F32) as psA,
        nc.psum_tensor("psB", [P, 2048], F32) as psB,
        nc.Block() as block,
    ):
        SBf = SBt[:, :].bitcast(F32)

        def bw(off, n, p0=0, p1=P):
            return SBt[p0:p1, off:off + n]

        def fv(off, nf, p0=0, p1=P):
            assert off % 2 == 0
            return SBf[p0:p1, off // 2:off // 2 + nf]

        R = Rec()
        sT, sV, sS = Sem(nc, "sT"), Sem(nc, "sV"), Sem(nc, "sS")
        sQ, sG = Sem(nc, "sQ"), Sem(nc, "sG")

        lastuse = {}

        def reg_dep(name):
            return list(lastuse.get(name, lastuse.get("__bar__", [])))

        def reg_use(name, sem, val):
            lastuse[name] = [(sem, val)]

        def barrier():
            snap = [(sT, sT.n), (sV, sV.n), (sS, sS.n)]
            for e in ENGINES:
                R.wait(e, *snap)
            for k in list(lastuse.keys()):
                lastuse[k] = list(snap)
            lastuse["__bar__"] = list(snap)

        def pA(i):
            return psA[:, i * 512:(i + 1) * 512]

        def pB(i):
            return psB[:, i * 512:(i + 1) * 512]

        def xv(ic, t0, n):
            return SBf[:, ic * T + t0: ic * T + t0 + n]

        PO = {}
        _next = [W_PERM]
        for nm, wds in [("allones", P), ("lnones", P), ("onesf", 2 * P),
                        ("idf", 2 * P), ("s1", 2 * NCC), ("s2", 2 * NCC),
                        ("sh1", 2 * NCC), ("sh2", 2 * NCC), ("sw3h", 2 * DC),
                        ("bdk", 2), ("bdv", 2), ("buk", 2 * DC), ("bo", 2 * DC),
                        ("bq", 2 * DC), ("bk", 2 * DC), ("bproj", 2 * DC),
                        ("l1b", 2 * DC), ("c_eps5", 2), ("c_eps6", 2)]:
            off = _next[0]
            assert off + wds <= WTOT
            _next[0] += wds
            PO[nm] = off

        def scol(name, j):
            return SBf[:, PO[name] // 2 + j: PO[name] // 2 + j + 1]

        g_id = R.E("gpsimd", "dma_start", fv(PO["idf"], P), ap["idf"][:, :], inc=sG, incn=16)
        for nm, src, n in [("s1", "s1c", NCC), ("s2", "s2c", NCC), ("sh1", "sh1c", NCC),
                           ("sh2", "sh2c", NCC), ("buk", "bukc", DC), ("bo", "boc", DC),
                           ("bdk", "bdkc", 1), ("bdv", "bdvc", 1), ("bq", "bqc", DC),
                           ("bk", "bkc", DC), ("bproj", "bprojc", DC), ("l1b", "l1bc", DC)]:
            R.E("gpsimd", "dma_start", fv(PO[nm], n), ap[src][:, :], inc=sG, incn=16)
        g_sw3 = R.E("gpsimd", "dma_start", fv(PO["sw3h"], DC), ap["sw3c"][:, :], inc=sG, incn=16)
        v_sw3 = R.E("vector", "tensor_scalar", fv(PO["sw3h"], DC), fv(PO["sw3h"], DC),
                    0.5, None, OP.mult, waits=[(sG, g_sw3)], inc=sV)
        R.E("vector", "memset", bw(PO["allones"], P), 1.0, inc=sV)
        R.E("vector", "memset", fv(PO["onesf"], P), 1.0, inc=sV)
        v_ones = R.E("vector", "memset", bw(PO["lnones"], P), 1.0 / 1024.0, inc=sV)
        R.E("vector", "memset", fv(PO["c_eps5"], 1), 1e-5, inc=sV)
        R.E("vector", "memset", fv(PO["c_eps6"], 1), 1e-6, inc=sV)
        nc.const_aps.register(F32, 1e-5, fv(PO["c_eps5"], 1))
        nc.const_aps.register(F32, 1e-6, fv(PO["c_eps6"], 1))

        def cast2d(dst, src, rows, nsplit):
            per = rows // nsplit
            v = None
            for i in range(nsplit):
                v = R.E("gpsimd", "dma_start", dst[i * per:(i + 1) * per, :],
                        src[i * per:(i + 1) * per, :], inc=sG, incn=16)
            return v

        def cast2d_parts(dst, src, rows, nsplit):
            per = rows // nsplit
            vs = []
            for i in range(nsplit):
                vs.append(R.E("gpsimd", "dma_start", dst[i * per:(i + 1) * per, :],
                              src[i * per:(i + 1) * per, :], inc=sG, incn=16))
            return vs

        g_w1p = cast2d_parts(w1bf, ap["w1"], DHID, 4)
        g_w1 = g_w1p[-1]
        g_w2p = cast2d_parts(w2bf, ap["w2"], DHID, 4)
        g_w2 = g_w2p[-1]
        g_h1p = cast2d_parts(h1bf, ap["h1"], DHID, 16)
        g_h1 = g_h1p[-1]
        g_h2p = cast2d_parts(h2bf, ap["h2"], DHID, 16)
        g_h2 = g_h2p[-1]
        g_w3 = cast2d(w3bf, ap["w3"], D, 4)

        R.phase = "xload"
        stg = [fv(W_TMP, D), fv(W_TMP + 2048, D)]
        ev_last = {}
        for tt in range(16):
            sl = tt % 2
            q = R.E("sync", "dma_start", stg[sl], ap["x"][tt * P:(tt + 1) * P, :],
                    waits=reg_dep(f"stg{sl}"), inc=sQ, incn=16)
            for ic in range(DC):
                bk = (tt * DC + ic) % 4
                tv = R.E("tensor", "transpose", pA(bk)[:, 0:P],
                         stg[sl][:, ic * P:(ic + 1) * P], fv(PO["idf"], P),
                         waits=[(sQ, q), (sG, g_id)] +
                         ([(sV, ev_last[bk])] if bk in ev_last else []),
                         inc=sT)
                ev_last[bk] = R.E("vector", "tensor_copy", xv(ic, tt * P, P),
                                  pA(bk)[:, 0:P], waits=[(sT, tv)], inc=sV)
            reg_use(f"stg{sl}", sT, sT.n)
        barrier()

        tmpM = fv(W_TMP, TC)
        tmpV_ = fv(W_TMP + 1024, TC)
        tmpR = fv(W_TMP + 2048, TC)
        tmpD = fv(W_TMP + 3072, TC)

        def ln_tc(tc, dst_fn):
            R.phase = "ln"
            t0 = tc * TC
            dep = reg_dep("tmp") + reg_dep("lnps")
            cast_done = []
            for ic in range(DC):
                c = R.E("vector", "tensor_copy", dst_fn(ic), xv(ic, t0, TC),
                        waits=(dep if ic == 0 else []), inc=sV)
                cast_done.append(c)
            for ic in range(DC):
                R.E("tensor", "matmul", pA(0), bw(PO["lnones"], P), dst_fn(ic),
                    start=(ic == 0), stop=(ic == DC - 1),
                    waits=[(sV, cast_done[ic]), (sV, v_ones)] + (dep if ic == 0 else []),
                    inc=sT if ic == DC - 1 else None)
            t_mean = sT.n
            sq_mm = {}
            for ic in range(DC):
                sqo = W_SQ if ic % 2 == 0 else W_SQ + TC
                w = ([(sT, sq_mm[ic - 2])] if ic >= 2 else []) + \
                    (reg_dep("sq") if ic < 2 else [])
                sv = R.E("scalar", "activation", bw(sqo, TC), xv(ic, t0, TC), AF.Square,
                         waits=w + (dep if ic == 0 else []), inc=sS)
                sq_mm[ic] = R.E("tensor", "matmul", pA(1), bw(PO["lnones"], P), bw(sqo, TC),
                                start=(ic == 0), stop=(ic == DC - 1),
                                waits=[(sS, sv)], inc=sT)
            t_sq = sT.n
            reg_use("sq", sT, t_sq)
            sm = R.E("scalar", "copy", tmpM, pA(0), waits=[(sT, t_mean)], inc=sS)
            v1 = R.E("vector", "tensor_mul", tmpV_, tmpM, tmpM, waits=[(sS, sm)], inc=sV)
            v2 = R.E("vector", "tensor_sub", tmpV_, pA(1), tmpV_,
                     waits=[(sT, t_sq), (sV, v1)], inc=sV)
            s1_ = R.E("scalar", "activation", tmpR, tmpV_, AF.Ln, bias=1e-5,
                      waits=[(sV, v2)], inc=sS)
            s2_ = R.E("scalar", "activation", tmpR, tmpR, AF.Exp, scale=-0.5,
                      waits=[(sS, s1_)], inc=sS)
            last = None
            for ic in range(DC):
                d1 = R.E("vector", "tensor_sub", tmpD, xv(ic, t0, TC), tmpM,
                         waits=[(sS, s2_)], inc=sV)
                last = R.E("vector", "tensor_mul", dst_fn(ic), tmpD, tmpR,
                           waits=[(sV, d1)], inc=sV)
            reg_use("tmp", sV, last)
            reg_use("lnps", sV, last)
            return last

        def ffn0_app():
            OFF_XN, OFF_H, OFF_G = W_R1, W_R2, W_R3
            SLAB = [W_R4 + i * 4096 for i in range(4)]

            def tr3(off, n):
                return bw(off, n).rearrange("p (m r) -> p m r", r=P)

            for tc in range(NTC):
                ln_done = ln_tc(tc, lambda ic: bw(OFF_XN + ic * TC, TC))

                R.phase = "f0_w12"
                w1d, w2d = {}, {}

                def ld_w12(oc):
                    d1 = reg_dep(f"fs0_{oc % 4}")
                    w1d[oc] = R.E("sync", "dma_start", tr3(SLAB[0] + (oc % 4) * 1024, 1024),
                                  w1bf[oc * P:(oc + 1) * P, :], transpose=True,
                                  waits=d1 + [(sG, g_w1p[oc // 8])], inc=sQ, incn=16)
                    d2 = reg_dep(f"fs1_{oc % 4}")
                    w2d[oc] = R.E("sync", "dma_start", tr3(SLAB[1] + (oc % 4) * 1024, 1024),
                                  w2bf[oc * P:(oc + 1) * P, :], transpose=True,
                                  waits=d2 + [(sG, g_w2p[oc // 8])], inc=sQ, incn=16)

                for oc in range(3):
                    ld_w12(oc)
                sil = fv(W_TMP, TC)
                for oc in range(NCC):
                    if oc + 3 < NCC:
                        ld_w12(oc + 3)
                    ba = oc % 2
                    pa, pb_ = pA(ba), pA(2 + ba)
                    dep_a = reg_dep(f"fpa{ba}")
                    dep_b = reg_dep(f"fpb{ba}")
                    for ic in range(DC):
                        R.E("tensor", "matmul", pa,
                            bw(SLAB[0] + (oc % 4) * 1024 + ic * P, P),
                            bw(OFF_XN + ic * TC, TC),
                            start=(ic == 0), stop=(ic == DC - 1),
                            waits=([(sQ, w1d[oc]), (sV, ln_done)] + dep_a)
                            if ic == 0 else [],
                            inc=sT if ic == DC - 1 else None)
                    ta = sT.n
                    for ic in range(DC):
                        R.E("tensor", "matmul", pb_,
                            bw(SLAB[1] + (oc % 4) * 1024 + ic * P, P),
                            bw(OFF_XN + ic * TC, TC),
                            start=(ic == 0), stop=(ic == DC - 1),
                            waits=([(sQ, w2d[oc])] + dep_b) if ic == 0 else [],
                            inc=sT if ic == DC - 1 else None)
                    tb = sT.n
                    reg_use(f"fs0_{oc % 4}", sT, ta)
                    reg_use(f"fs1_{oc % 4}", sT, tb)
                    ss = R.E("scalar", "activation", sil, pa, AF.Silu,
                             scale=scol("s1", oc),
                             waits=[(sT, ta)] + (reg_dep("tmp") if oc == 0 else []),
                             inc=sS)
                    hv = R.E("vector", "scalar_tensor_tensor", bw(OFF_H + oc * TC, TC),
                             pb_, scol("s2", oc), sil, OP.mult, OP.mult,
                             waits=[(sT, tb), (sS, ss)], inc=sV)
                    reg_use(f"fpa{ba}", sS, ss)
                    reg_use(f"fpb{ba}", sV, hv)
                reg_use("tmp", sV, sV.n)
                h_done = sV.n

                def big_phase(wdram, g_cast, OFF_IN, OFF_OUT, scname, n_oc, act,
                              tag, in_sem, in_val):
                    bd = {}

                    def ld(oc):
                        d = reg_dep(f"{tag}_{oc % 4}")
                        gv = g_cast[oc // 2] if isinstance(g_cast, list) else g_cast
                        bd[oc] = R.E("sync", "dma_start", tr3(SLAB[oc % 4], 4096),
                                     wdram[oc * P:(oc + 1) * P, :], transpose=True,
                                     waits=d + [(sG, gv)], inc=sQ, incn=16)

                    for oc in range(min(3, n_oc)):
                        ld(oc)
                    for op in range(n_oc // 2):
                        for j in range(2):
                            oc = op * 2 + j
                            if oc + 3 < n_oc:
                                ld(oc + 3)
                            pj = pA(j) if op % 2 == 0 else pA(2 + j)
                            dep_ps = reg_dep(f"{tag}p{j}{op % 2}")
                            for cc in range(NCC):
                                R.E("tensor", "matmul", pj,
                                    bw(SLAB[oc % 4] + cc * P, P),
                                    bw(OFF_IN + cc * TC, TC),
                                    start=(cc == 0), stop=(cc == NCC - 1),
                                    waits=([(sQ, bd[oc]), (in_sem, in_val)] + dep_ps)
                                    if cc == 0 else [],
                                    inc=sT if cc == NCC - 1 else None)
                            tj = sT.n
                            reg_use(f"{tag}_{oc % 4}", sT, tj)
                            if act == "gelu":
                                ev = R.E("scalar", "activation", bw(OFF_OUT + oc * TC, TC),
                                         pj, AF.Gelu, scale=scol(scname, oc),
                                         waits=[(sT, tj)], inc=sS)
                                reg_use(f"{tag}p{j}{op % 2}", sS, ev)
                            else:
                                ev = R.E("vector", "scalar_tensor_tensor",
                                         xv(oc, tc * TC, TC), pj, scol("sw3h", oc),
                                         xv(oc, tc * TC, TC), OP.mult, OP.add,
                                         waits=[(sT, tj), (sV, v_sw3)], inc=sV)
                                reg_use(f"{tag}p{j}{op % 2}", sV, ev)

                R.phase = "f0_h1"
                big_phase(h1bf, g_h1p, OFF_H, OFF_G, "sh1", NCC, "gelu", "fh1", sV, h_done)
                g1_done = sS.n
                R.phase = "f0_h2"
                big_phase(h2bf, g_h2p, OFF_G, OFF_H, "sh2", NCC, "gelu", "fh2", sS, g1_done)
                g2_done = sS.n
                R.phase = "f0_w3"
                big_phase(w3bf, g_w3, OFF_H, None, None, DC, "xadd", "fw3", sS, g2_done)

        def l0_attn():
            Q_, K_ = W_R1, W_R2
            ln_last = None
            for tc in range(NTC):
                ln_last = ln_tc(tc, lambda ic, tc=tc: bw(Q_ + ic * T + tc * TC, TC))
            spl = R.E("sync", "dma_start", xsp[:, :], SBf[:, 0:DC * T],
                      waits=[(sV, ln_last), (sS, sS.n)], inc=sQ, incn=16)
            dep = [(sQ, spl)]
            R.E("gpsimd", "dma_start", bw(XA_WDK, 1024),
                ap["wdk"].rearrange("(ic p) c -> p ic c", p=P), waits=dep, inc=sG, incn=16)
            R.E("gpsimd", "dma_start", bw(XA_WDV, 1024),
                ap["wdv"].rearrange("(ic p) c -> p ic c", p=P), inc=sG, incn=16)
            R.E("gpsimd", "dma_start", bw(XA_WUK, 1024), ap["wuk"][:, :], inc=sG, incn=16)
            R.E("gpsimd", "dma_start", bw(XA_WUV, 1024), ap["wuv"][:, :], inc=sG, incn=16)
            R.E("gpsimd", "dma_start", bw(XA_COS, T), ap["cosb"][:, :], inc=sG, incn=16)
            R.E("gpsimd", "dma_start", bw(XA_SIN, T), ap["sinb"][:, :], inc=sG, incn=16)
            R.E("gpsimd", "dma_start", bw(XA_PROT, P), ap["prot"][:, :], inc=sG, incn=16)
            g_sm = R.E("gpsimd", "dma_start", bw(XA_BUV, D, 0, 1), ap["buv_row"][:, :],
                       inc=sG, incn=16)
            z1 = R.E("vector", "memset", bw(XA_QEXT, 2 * T), 0.0, waits=dep, inc=sV)
            z2 = R.E("vector", "memset", bw(XA_KEXT, 256), 0.0, waits=dep, inc=sV)
            R.E("vector", "memset", bw(XA_KEXT, P, 64, 65), 1.0, waits=[(sV, z2)], inc=sV)
            z3 = R.E("vector", "memset", bw(XA_KEXT + P, P, 32, 33), 1.0,
                     waits=[(sV, z2)], inc=sV)
            vz = None
            for tt in range(16):
                onescol = bw(W_R4 + tt * 1040 + 64, 8 * 129).rearrange(
                    "p (h d) -> p h d", d=129)[:, :, 0:1]
                vz = R.E("vector", "memset", onescol, 1.0,
                         waits=reg_dep("vbufz") if tt == 0 else [], inc=sV)

            R.phase = "a0_ckv"
            for (dst, wofs, bname) in ((XA_CK, XA_WDK, "bdk"), (XA_CV, XA_WDV, "bdv")):
                for tc in range(NTC):
                    pj = pA(tc % 2)
                    depp = reg_dep(f"ckps{tc % 2}")
                    for ic in range(DC):
                        R.E("tensor", "matmul", pj, bw(wofs + ic * P, P),
                            bw(Q_ + ic * T + tc * TC, TC),
                            start=(ic == 0), stop=(ic == DC - 1),
                            waits=([(sG, g_sm), (sV, ln_last)] + depp) if ic == 0 else [],
                            inc=sT if ic == DC - 1 else None)
                    ev = R.E("scalar", "activation", bw(dst + tc * TC, TC), pj, AF.Identity,
                             bias=scol(bname, 0), waits=[(sT, sT.n)], inc=sS)
                    reg_use(f"ckps{tc % 2}", sS, ev)

            R.phase = "a0_kT"
            ck_done = sS.n
            for oc in range(DC):
                for tc in range(NTC):
                    pj = pA(2 + (tc % 2))
                    depp = reg_dep(f"kps{tc % 2}")
                    R.E("tensor", "matmul", pj, bw(XA_WUK + oc * P, P),
                        bw(XA_CK + tc * TC, TC), start=True, stop=True,
                        waits=[(sS, ck_done)] + depp, inc=sT)
                    ev = R.E("scalar", "activation", bw(K_ + oc * T + tc * TC, TC), pj,
                             AF.Identity, bias=scol("buk", oc), waits=[(sT, sT.n)], inc=sS)
                    reg_use(f"kps{tc % 2}", sS, ev)
            kT_done = sS.n

            R.phase = "a0_v"
            for tt in range(16):
                for og in range(2):
                    pj = pB(og)
                    depp = reg_dep(f"vps{og}")
                    R.E("tensor", "matmul", pj, bw(XA_CV + tt * P, P),
                        bw(XA_WUV + og * TC, TC), start=True, stop=False,
                        waits=([(sS, ck_done)] + depp), inc=None)
                    R.E("tensor", "matmul", pj, bw(PO["allones"], P, 0, 1),
                        bw(XA_BUV + og * TC, TC, 0, 1), start=False, stop=True,
                        waits=[(sG, g_sm)], inc=sT)
                    tj = sT.n
                    base = W_R4 + tt * 1040 + og * 4 * 129
                    pview = psB[:, og * 512: og * 512 + 512].rearrange(
                        "p (h e d) -> p h e d", h=4, e=2, d=64)
                    dve = bw(base, 4 * 129).rearrange("p (h d) -> p h d", d=129)[:, :, 0:64]
                    dvo = bw(base + 65, 4 * 129).rearrange(
                        "p (h d) -> p h d", d=129)[:, :, 0:64]
                    R.E("vector", "tensor_copy", dve, pview[:, :, 0, :],
                        waits=[(sT, tj), (sV, vz)], inc=sV)
                    ev2 = R.E("vector", "tensor_copy", dvo, pview[:, :, 1, :],
                              waits=[(sT, tj)], inc=sV)
                    reg_use(f"vps{og}", sV, ev2)
            v_done = sV.n
            t_after_v = sT.n

            R.phase = "a0_rope"
            rt1 = fv(W_TMP, TC)
            rt2 = fv(W_TMP + 1024, TC)
            rope_start = [(sT, t_after_v), (sS, kT_done), (sV, v_done)] + reg_dep("tmp")
            last_rope = None
            for (buf, _) in ((Q_, "q"), (K_, "k")):
                for fc in range(DC):
                    for tc in range(NTC):
                        pj = pB(2 + (tc % 2))
                        depp = reg_dep(f"rps{tc % 2}")
                        tv = R.E("tensor", "matmul", pj, bw(XA_PROT, P),
                                 bw(buf + fc * T + tc * TC, TC), start=True, stop=True,
                                 waits=[(sG, g_sm)] + depp +
                                 (rope_start if last_rope is None else []),
                                 inc=sT)
                        a1 = R.E("vector", "tensor_mul", rt1, bw(buf + fc * T + tc * TC, TC),
                                 bw(XA_COS + tc * TC, TC),
                                 waits=([(sG, g_sm)] + (rope_start if last_rope is None else [])),
                                 inc=sV)
                        a2 = R.E("vector", "tensor_mul", rt2, pj, bw(XA_SIN + tc * TC, TC),
                                 waits=[(sT, tv), (sV, a1)], inc=sV)
                        last_rope = R.E("vector", "tensor_add",
                                        bw(buf + fc * T + tc * TC, TC), rt1, rt2,
                                        waits=[(sV, a2)], inc=sV)
                        reg_use(f"rps{tc % 2}", sV, last_rope)
            reg_use("tmp", sV, last_rope)
            rope_done = last_rope

            statf = fv(W_STAT, 96)
            stats_done = {}

            def p1_subunits(h):
                fc, par = h // 2, h % 2
                p0 = par * 64
                mcol = 64 if par == 0 else 32
                QEXT = XA_QEXT + par * T
                for t1t in range(16):
                    for t2h in range(2):
                        dep = reg_dep("p1sAB") + ([(sV, rope_done)]
                                                  if h == 0 and t1t == 0 else [])
                        for n in range(2):
                            R.E("tensor", "matmul",
                                psA[:, 1024 + n * 512:1536 + n * 512],
                                bw(Q_ + fc * T + t1t * P, P, p0, p0 + 64),
                                bw(K_ + fc * T + t2h * 1024 + n * 512, 512, p0, p0 + 64),
                                start=True, stop=True,
                                waits=dep if n == 0 else [],
                                inc=sT if n == 1 else None)
                        tS = sT.n
                        red = R.E("vector", "tensor_reduce", statf[:, 66 + t2h:67 + t2h],
                                  psA[:, 1024:2048], AX.X, OP.max, negate=True,
                                  waits=[(sT, tS)], inc=sV)
                        reg_use("p1sAB", sV, red)
                        if t2h == 1:
                            cm = R.E("vector", "tensor_tensor", statf[:, mcol:mcol + 1],
                                     statf[:, 66:67], statf[:, 67:68], OP.min,
                                     waits=[(sV, red)], inc=sV)
                            tr = R.E("tensor", "transpose", psA[0:65, 1024:1024 + P],
                                     statf[:, 0:65], fv(PO["idf"], P),
                                     waits=[(sV, cm)] + reg_dep("stps"), inc=sT)
                            reg_use("p1sAB", sT, tr)
                            cp = R.E("scalar", "copy",
                                     bw(QEXT + t1t * P, P, mcol, mcol + 1),
                                     psA[mcol:mcol + 1, 1024:1024 + P],
                                     waits=[(sT, tr)], inc=sS)
                            reg_use("stps", sS, cp)
                            reg_use("p1sAB", sS, cp)
                        yield

            R.phase = "a0_pass1"
            for _ in p1_subunits(0):
                pass
            stats_done[0] = sS.n

            for h in range(NH):
                fc, par = h // 2, h % 2
                p0 = par * 64
                mcol = 64 if par == 0 else 32
                QEXT = XA_QEXT + par * T
                KEXT = XA_KEXT + par * P
                hp = h // 2
                gen = p1_subunits(h + 1) if h + 1 < NH else None
                qc = R.E("scalar", "copy", bw(QEXT, T, p0, p0 + 64),
                         bw(Q_ + fc * T, T, p0, p0 + 64),
                         waits=[(sV, z1), (sS, stats_done[h])], inc=sS)
                R.phase = "a0_pass2"
                for t1h in range(2):
                    avb = (0 if t1h == 0 else 2)
                    dnb = (2 if t1h == 0 else 0)
                    av_dep = reg_dep(f"avps{avb}") + reg_dep(f"avps{avb + 1}")
                    dn_dep = reg_dep(f"avps{dnb}") + reg_dep(f"avps{dnb + 1}")
                    av_inc = {}
                    exp_prev = None
                    tSt_prev = None
                    for t2t in range(16):
                        kc = R.E("vector", "tensor_copy", bw(KEXT, P, p0, p0 + 64),
                                 bw(K_ + fc * T + t2t * P, P, p0, p0 + 64),
                                 waits=[(sV, z3)] +
                                 ([(sT, tSt_prev)] if tSt_prev else reg_dep("kext")),
                                 inc=sV)
                        for n in range(2):
                            R.E("tensor", "matmul",
                                psA[:, n * 512:(n + 1) * 512], bw(KEXT, P),
                                bw(QEXT + t1h * 1024 + n * 512, 512),
                                start=True, stop=True,
                                waits=([(sV, kc), (sS, qc)] +
                                       ([(sS, exp_prev)] if exp_prev else []))
                                if n == 0 else [],
                                inc=sT if n == 1 else None)
                        tSt = sT.n
                        tSt_prev = tSt
                        slot = XA_EXPT + (t2t % 2) * 1024
                        ex = R.E("scalar", "activation", bw(slot, 1024),
                                 psA[:, 0:1024], AF.Exp, scale=0.125,
                                 waits=[(sT, tSt)] +
                                 ([(sT, av_inc[t2t - 2])] if t2t >= 2 else []),
                                 inc=sS)
                        exp_prev = ex
                        blk = W_R4 + t2t * 1040 + hp * 129
                        if par == 0:
                            lhs = SBt[:, blk: blk + 65]

                            def avout(n, avb=avb):
                                return psB[0:65, (avb + n) * 512:(avb + n + 1) * 512]
                        else:
                            lhs = SBt[:, blk + 1: blk + 129]

                            def avout(n, avb=avb):
                                return psB[0:128, (avb + n) * 512:(avb + n + 1) * 512]
                        last = None
                        for n in range(2):
                            last = R.E("tensor", "matmul", avout(n), lhs,
                                       bw(slot + n * 512, 512),
                                       start=(t2t == 0), stop=(t2t == 15),
                                       waits=([(sS, ex), (sV, v_done)] +
                                              (av_dep if t2t == 0 else []))
                                       if n == 0 else [],
                                       inc=sT if (par == 0 and n == 1) else None)
                        if par == 1:
                            for n in range(2):
                                last = R.E("tensor", "matmul",
                                           psB[0:1, (dnb + n) * 512:(dnb + n + 1) * 512],
                                           bw(PO["allones"], 1), bw(slot + n * 512, 512),
                                           start=(t2t == 0), stop=(t2t == 15),
                                           waits=(dn_dep if t2t == 0 and n == 0 else []),
                                           inc=sT if n == 1 else None)
                        av_inc[t2t] = last
                        if gen is not None:
                            next(gen, None)
                    t_av = sT.n
                    R.phase = "a0_norm"
                    for n in range(2):
                        if par == 0:
                            drow, dslice = 64, psB[64:65, (avb + n) * 512:(avb + n + 1) * 512]
                        else:
                            drow, dslice = 0, psB[0:1, (dnb + n) * 512:(dnb + n + 1) * 512]
                        rc = R.E("vector", "reciprocal",
                                 fv(XA_REC, 512, drow, drow + 1), dslice,
                                 waits=[(sT, t_av)] + reg_dep("recrow"), inc=sV)
                        bc = R.E("tensor", "matmul", psB[:, (dnb + n) * 512:(dnb + n + 1) * 512],
                                 fv(PO["onesf"], P, drow, drow + 1),
                                 fv(XA_REC, 512, drow, drow + 1), start=True, stop=True,
                                 waits=[(sV, rc)] + reg_dep(f"avps{dnb + n}"), inc=sT)
                        cb = R.E("scalar", "copy", fv(XA_RECB, 512),
                                 psB[:, (dnb + n) * 512:(dnb + n + 1) * 512],
                                 waits=[(sT, bc)] + reg_dep("recbb"), inc=sS)
                        ao = R.E("vector", "tensor_mul",
                                 bw(Q_ + fc * T + t1h * 1024 + n * 512, 512, p0, p0 + 64),
                                 psB[p0:p0 + 64, (avb + n) * 512:(avb + n + 1) * 512],
                                 fv(XA_RECB, 512, p0, p0 + 64),
                                 waits=[(sS, cb)], inc=sV)
                        reg_use("recrow", sV, ao)
                        reg_use("recbb", sV, ao)
                        reg_use(f"avps{avb + n}", sV, ao)
                        reg_use(f"avps{dnb + n}", sV, ao)
                    reg_use("kext", sT, t_av)
                    R.phase = "a0_pass2"
                if gen is not None:
                    for _ in gen:
                        pass
                    stats_done[h + 1] = sS.n
            aout_done = sV.n

            xr = R.E("sync", "dma_start", SBf[:, 0:DC * T], xsp[:, :],
                     waits=[(sV, aout_done), (sS, sS.n), (sT, sT.n)], inc=sQ, incn=16)
            g_wo = R.E("gpsimd", "dma_start", bw(W_R4, DC * D),
                       ap["wo"].rearrange("(ic p) o -> p ic o", p=P),
                       waits=[(sT, sT.n)], inc=sG, incn=16)
            R.phase = "a0_wo"
            for tc in range(NTC):
                for oc in range(DC):
                    k = (tc * DC + oc) % 2
                    pj = pA(k)
                    depp = reg_dep(f"wops{k}")
                    for ic in range(DC):
                        R.E("tensor", "matmul", pj, bw(W_R4 + ic * D + oc * P, P),
                            bw(Q_ + ic * T + tc * TC, TC),
                            start=(ic == 0), stop=(ic == DC - 1),
                            waits=([(sG, g_wo), (sV, aout_done), (sQ, xr)] + depp)
                            if ic == 0 else [], inc=sT if ic == DC - 1 else None)
                    tj = sT.n
                    ev = R.E("vector", "scalar_tensor_tensor", xv(oc, tc * TC, TC),
                             pj, scol("bo", oc), xv(oc, tc * TC, TC), OP.add, OP.add,
                             waits=[(sT, tj)], inc=sV)
                    reg_use(f"wops{k}", sV, ev)

        def l1_ffn():
            OFF_XN = W_R1
            OFF_H1 = W_R1 + 4096
            OFF_H2 = W_R1 + 8192
            SLW = [W_R2, W_R3, W_R4]
            gw = []
            for i, nm in enumerate(("l1w", "l1a1", "l1a2")):
                g = R.E("gpsimd", "dma_start", bw(SLW[i], DC * D),
                        ap[nm].rearrange("(ic p) o -> p ic o", p=P),
                        waits=reg_dep(f"l1w{i}"), inc=sG, incn=16)
                gw.append(g)
            tanh_t = bw(W_XTR, TC)
            for tc in range(NTC):
                ln_done = ln_tc(tc, lambda ic: bw(OFF_XN + ic * TC, TC))
                R.phase = "l1ffn"
                stages = [(SLW[0], gw[0], OFF_XN, OFF_H1, "copy"),
                          (SLW[1], gw[1], OFF_H1, OFF_H2, "copy"),
                          (SLW[2], gw[2], OFF_H2, None, "tanh")]
                prev = (sV, ln_done)
                for (slw, g, src, dst, mode) in stages:
                    for oc in range(DC):
                        k = oc % 2
                        pj = pA(2 + k)
                        depp = reg_dep(f"l1ps{k}")
                        for ic in range(DC):
                            R.E("tensor", "matmul", pj, bw(slw + ic * D + oc * P, P),
                                bw(src + ic * TC, TC),
                                start=(ic == 0), stop=(ic == DC - 1),
                                waits=([(sG, g), prev] + depp) if ic == 0 else [],
                                inc=sT if ic == DC - 1 else None)
                        tj = sT.n
                        if mode == "copy":
                            ev = R.E("scalar", "copy", bw(dst + oc * TC, TC), pj,
                                     waits=[(sT, tj)], inc=sS)
                            reg_use(f"l1ps{k}", sS, ev)
                        else:
                            th = R.E("scalar", "activation", tanh_t, pj, AF.Tanh,
                                     bias=scol("l1b", oc),
                                     waits=[(sT, tj)] + reg_dep("tanht"), inc=sS)
                            ev = R.E("vector", "scalar_tensor_tensor",
                                     xv(oc, tc * TC, TC), tanh_t, 0.5,
                                     xv(oc, tc * TC, TC), OP.mult, OP.add,
                                     waits=[(sS, th)], inc=sV)
                            reg_use("tanht", sV, ev)
                            reg_use(f"l1ps{k}", sV, ev)
                    prev = (sS, sS.n) if mode == "copy" else (sV, sV.n)
                for i in range(3):
                    reg_use(f"l1w{i}", sT, sT.n)

        def l1_attn():
            XN, QT, KT = W_R1, W_R2, W_R3
            ln_last = None
            for tc in range(NTC):
                ln_last = ln_tc(tc, lambda ic, tc=tc: bw(XN + ic * T + tc * TC, TC))
            spl = R.E("sync", "dma_start", xsp[:, :], SBf[:, 0:DC * T],
                      waits=[(sV, ln_last), (sS, sS.n)], inc=sQ, incn=16)
            g_wq = R.E("gpsimd", "dma_start", bw(XL_S0, DC * D),
                       ap["wq"].rearrange("(ic p) o -> p ic o", p=P),
                       waits=[(sQ, spl)], inc=sG, incn=16)
            g_wk = R.E("gpsimd", "dma_start", bw(XL_S1, DC * D),
                       ap["wk"].rearrange("(ic p) o -> p ic o", p=P), inc=sG, incn=16)
            R.E("gpsimd", "dma_start", bw(W_FEAT, 256, 0, DH), ap["wqf"][:, :],
                waits=reg_dep("feat"), inc=sG, incn=16)
            g_f1 = R.E("gpsimd", "dma_start", bw(W_FEAT, 256, 64, P), ap["wqf"][:, :],
                       inc=sG, incn=16)
            R.E("gpsimd", "dma_start", bw(W_FEAT + 256, 256, 0, DH), ap["wkf"][:, :],
                inc=sG, incn=16)
            g_f2 = R.E("gpsimd", "dma_start", bw(W_FEAT + 256, 256, 64, P), ap["wkf"][:, :],
                       inc=sG, incn=16)
            g_bv = R.E("gpsimd", "dma_start", bw(XL_BV, D, 0, 1), ap["bv_row"][:, :],
                       inc=sG, incn=16)
            vz = None
            for tt in range(16):
                onescol = bw(W_R4 + tt * 1040 + 64, 8 * 129).rearrange(
                    "p (h d) -> p h d", d=129)[:, :, 0:1]
                vz = R.E("vector", "memset", onescol, 1.0,
                         waits=reg_dep("vbufz") if tt == 0 else [], inc=sV)

            R.phase = "l1_qk"

            def proj_fm(dst, wslot, gdep, bname):
                last = None
                for oc in range(DC):
                    for tc in range(NTC):
                        k = tc % 2
                        pj = pA(k)
                        depp = reg_dep(f"qkps{k}")
                        for ic in range(DC):
                            R.E("tensor", "matmul", pj, bw(wslot + ic * D + oc * P, P),
                                bw(XN + ic * T + tc * TC, TC),
                                start=(ic == 0), stop=(ic == DC - 1),
                                waits=([(sG, gdep), (sV, ln_last)] + depp) if ic == 0 else [],
                                inc=sT if ic == DC - 1 else None)
                        last = R.E("scalar", "activation", bw(dst + oc * T + tc * TC, TC),
                                   pj, AF.Identity, bias=scol(bname, oc),
                                   waits=[(sT, sT.n)], inc=sS)
                        reg_use(f"qkps{k}", sS, last)
                return last

            q_done = proj_fm(QT, XL_S0, g_wq, "bq")
            g_wv = R.E("gpsimd", "dma_start", bw(XL_S0, DC * D),
                       ap["wv"].rearrange("(ic p) o -> p ic o", p=P),
                       waits=[(sT, sT.n), (sS, q_done)], inc=sG, incn=16)
            k_done = proj_fm(KT, XL_S1, g_wk, "bk")

            R.phase = "l1_v"
            for tt in range(16):
                for og in range(2):
                    pj = pB(og)
                    depp = reg_dep(f"vps{og}")
                    for ic in range(DC):
                        R.E("tensor", "matmul", pj, bw(XN + ic * T + tt * P, P),
                            bw(XL_S0 + ic * D + og * TC, TC),
                            start=(ic == 0), stop=False,
                            waits=([(sG, g_wv)] + depp) if ic == 0 else [],
                            inc=None)
                    R.E("tensor", "matmul", pj, bw(PO["allones"], P, 0, 1),
                        bw(XL_BV + og * TC, TC, 0, 1), start=False, stop=True,
                        waits=[(sG, g_bv)], inc=sT)
                    tj = sT.n
                    base = W_R4 + tt * 1040 + og * 4 * 129
                    pview = psB[:, og * 512: og * 512 + 512].rearrange(
                        "p (h e d) -> p h e d", h=4, e=2, d=64)
                    dve = bw(base, 4 * 129).rearrange("p (h d) -> p h d", d=129)[:, :, 0:64]
                    dvo = SBt[:, base + 65: base + 65 + 4 * 129].rearrange(
                        "p (h d) -> p h d", d=129)[:, :, 0:64]
                    R.E("vector", "tensor_copy", dve, pview[:, :, 0, :],
                        waits=[(sT, tj), (sV, vz)], inc=sV)
                    ev2 = R.E("vector", "tensor_copy", dvo, pview[:, :, 1, :],
                              waits=[(sT, tj)], inc=sV)
                    reg_use(f"vps{og}", sV, ev2)
            v_done = sV.n

            zc = R.E("vector", "memset", bw(XL_CTX, NH * 260), 0.0,
                     waits=[(sS, k_done)], inc=sV)

            R.phase = "l1_ctx"
            KP = XN
            for h in range(NH):
                fc, par = h // 2, h % 2
                p0 = par * 64
                hp = h // 2
                for nt in range(16):
                    pj = pA(nt % 2)
                    depp = reg_dep(f"kfps{nt % 2}")
                    R.E("tensor", "matmul", pj[:, 0:256],
                        bw(KT + fc * T + nt * P, P, p0, p0 + 64),
                        bw(W_FEAT + 256, 256, p0, p0 + 64), start=True, stop=True,
                        waits=([(sG, g_f2), (sS, k_done), (sV, v_done)] + depp +
                               (reg_dep("kpbuf") if nt == 0 else [])),
                        inc=sT)
                    tj = sT.n
                    ex = R.E("scalar", "activation", bw(KP + nt * 256, 256), pj[:, 0:256],
                             AF.Exp, waits=[(sT, tj)], inc=sS)
                    reg_use(f"kfps{nt % 2}", sS, ex)
                    blk = W_R4 + nt * 1040 + hp * 129
                    rhs = SBt[:, blk: blk + 65] if par == 0 else SBt[:, blk + 1: blk + 129]
                    nw = 65 if par == 0 else 128
                    for fh in range(2):
                        R.E("tensor", "matmul",
                            psB[:, 1024 + fh * 512: 1024 + fh * 512 + nw],
                            bw(KP + nt * 256 + fh * P, P), rhs,
                            start=(nt == 0), stop=(nt == 15),
                            waits=([(sS, ex)] +
                                   (reg_dep("ctxps") if nt == 0 and fh == 0 else [])),
                            inc=sT if nt == 15 else None)
                t_ctx = sT.n
                reg_use("kpbuf", sT, t_ctx)
                cbase = XL_CTX + h * 260
                evs = None
                for fh in range(2):
                    if par == 0:
                        evs = R.E("vector", "tensor_copy", bw(cbase + fh * 130, 65),
                                  psB[:, 1024 + fh * 512: 1024 + fh * 512 + 65],
                                  waits=[(sT, t_ctx), (sV, zc)], inc=sV)
                    else:
                        evs = R.E("vector", "tensor_copy", bw(cbase + fh * 130 + 63, 65),
                                  psB[:, 1024 + fh * 512 + 63: 1024 + fh * 512 + 128],
                                  waits=[(sT, t_ctx), (sV, zc)], inc=sV)
                reg_use("ctxps", sV, evs)
            ctx_done = sV.n

            g_wp = R.E("gpsimd", "dma_start", bw(KT, DC * D),
                       ap["wproj"].rearrange("(ic p) o -> p ic o", p=P),
                       waits=[(sT, sT.n), (sV, ctx_done)], inc=sG, incn=16)

            R.phase = "l1_out"
            QP = KP + 4096
            for h in range(NH):
                fc, par = h // 2, h % 2
                p0 = par * 64
                mcol = 64 if par == 0 else 63
                cbase = XL_CTX + h * 260
                qp_last = None
                for fh in range(2):
                    for tc in range(NTC):
                        pj = pA(2 + (tc % 2))
                        depp = reg_dep(f"qfps{tc % 2}")
                        R.E("tensor", "matmul", pj,
                            bw(W_FEAT + fh * P, P, p0, p0 + 64),
                            bw(QT + fc * T + tc * TC, TC, p0, p0 + 64),
                            start=True, stop=True,
                            waits=([(sG, g_f1)] + depp +
                                   (reg_dep("qpbuf") if fh == 0 and tc == 0 else [])),
                            inc=sT)
                        qp_last = R.E("scalar", "activation", bw(QP + fh * T + tc * TC, TC),
                                      pj, AF.Exp, waits=[(sT, sT.n)], inc=sS)
                        reg_use(f"qfps{tc % 2}", sS, qp_last)
                qp_done = sS.n
                for nt4 in range(NTC):
                    nb = nt4 % 4
                    pj = psB[0:65, nb * 512:(nb + 1) * 512] if par == 0 else \
                        psB[0:128, nb * 512:(nb + 1) * 512]
                    depp = reg_dep(f"l1avps{nb}")
                    for fh in range(2):
                        lhs = (bw(cbase + fh * 130, 65) if par == 0
                               else bw(cbase + fh * 130, 128))
                        R.E("tensor", "matmul", pj, lhs, bw(QP + fh * T + nt4 * TC, TC),
                            start=(fh == 0), stop=(fh == 1),
                            waits=([(sS, qp_done)] + depp) if fh == 0 else [],
                            inc=sT if fh == 1 else None)
                    tj = sT.n
                    for fh in range(2):
                        kcol = (cbase + fh * 130 + 64) if par == 0 else \
                            (cbase + fh * 130 + 63)
                        R.E("tensor", "matmul", psA[0:1, nb * 512:(nb + 1) * 512],
                            bw(kcol, 1), bw(QP + fh * T + nt4 * TC, TC),
                            start=(fh == 0), stop=(fh == 1),
                            waits=(reg_dep(f"l1den{nb}") if fh == 0 else []),
                            inc=sT if fh == 1 else None)
                    t_den = sT.n
                    ad = R.E("scalar", "add", fv(XL_REC + nb * 1024, 512, 0, 1),
                             psA[0:1, nb * 512:(nb + 1) * 512],
                             1e-6, waits=[(sT, t_den)] + reg_dep(f"l1rec{nb}"), inc=sS)
                    rc = R.E("vector", "reciprocal", fv(XL_REC + nb * 1024, 512, 0, 1),
                             fv(XL_REC + nb * 1024, 512, 0, 1), waits=[(sS, ad)], inc=sV)
                    bc = R.E("tensor", "matmul", pA(2 + nt4 % 2),
                             fv(PO["onesf"], P, 0, 1),
                             fv(XL_REC + nb * 1024, 512, 0, 1), start=True, stop=True,
                             waits=[(sV, rc)] + reg_dep(f"qfps{nt4 % 2}"), inc=sT)
                    cb = R.E("scalar", "copy", fv(XL_RECB + (nt4 % 2) * 1024, 512),
                             pA(2 + nt4 % 2),
                             waits=[(sT, bc)] + reg_dep(f"l1recbb{nt4 % 2}"), inc=sS)
                    reg_use(f"qfps{nt4 % 2}", sS, cb)
                    ao = R.E("vector", "tensor_mul",
                             bw(QT + fc * T + nt4 * TC, TC, p0, p0 + 64),
                             psB[p0:p0 + 64, nb * 512:(nb + 1) * 512],
                             fv(XL_RECB + (nt4 % 2) * 1024, 512, p0, p0 + 64),
                             waits=[(sS, cb)], inc=sV)
                    reg_use(f"l1rec{nb}", sV, ao)
                    reg_use(f"l1den{nb}", sV, rc)
                    reg_use(f"l1recbb{nt4 % 2}", sV, ao)
                    reg_use(f"l1avps{nb}", sV, ao)
                reg_use("qpbuf", sV, sV.n)
            aout_done = sV.n

            R.phase = "l1_proj"
            xr = R.E("sync", "dma_start", SBf[:, 0:DC * T], xsp[:, :],
                     waits=[(sV, aout_done), (sS, sS.n), (sT, sT.n)], inc=sQ, incn=16)

            for tc in range(NTC):
                for oc in range(DC):
                    k = (tc * DC + oc) % 2
                    pj = pA(2 + k)
                    depp = reg_dep(f"pjps{k}")
                    for ic in range(DC):
                        R.E("tensor", "matmul", pj, bw(KT + ic * D + oc * P, P),
                            bw(QT + ic * T + tc * TC, TC),
                            start=(ic == 0), stop=(ic == DC - 1),
                            waits=([(sG, g_wp), (sV, aout_done), (sQ, xr)] + depp)
                            if ic == 0 else [], inc=sT if ic == DC - 1 else None)
                    tj = sT.n
                    ev = R.E("vector", "scalar_tensor_tensor", xv(oc, tc * TC, TC),
                             pj, scol("bproj", oc), xv(oc, tc * TC, TC), OP.add, OP.add,
                             waits=[(sT, tj)], inc=sV)
                    reg_use(f"pjps{k}", sV, ev)

        if steps >= 1:
            ffn0_app()
            barrier()
        if steps >= 2:
            l0_attn()
            barrier()
        if steps >= 3:
            ffn0_app()
            barrier()
        if steps >= 4:
            l1_ffn()
            barrier()
        if steps >= 5:
            l1_attn()
            barrier()
        if steps >= 6:
            l1_ffn()
            barrier()

        R.phase = "xstore"
        ev_last = {}
        for tt in range(16):
            sl = tt % 2
            last_ev = None
            for ic in range(DC):
                bk = (tt * DC + ic) % 4
                dep = reg_dep(f"ostg{sl}") if ic == 0 else []
                tv = R.E("tensor", "transpose", pA(bk)[:, 0:P],
                         xv(ic, tt * P, P), fv(PO["idf"], P),
                         waits=dep + ([(sV, ev_last[bk])] if bk in ev_last else []),
                         inc=sT)
                last_ev = R.E("vector", "tensor_copy", stg[sl][:, ic * P:(ic + 1) * P],
                              pA(bk)[:, 0:P], waits=[(sT, tv)], inc=sV)
                ev_last[bk] = last_ev
            q = R.E("sync", "dma_start", out[tt * P:(tt + 1) * P, :], stg[sl],
                    waits=[(sV, last_ev)], inc=sQ, incn=16)
            reg_use(f"ostg{sl}", sQ, q)
        R.wait("sync", (sQ, sQ.n))

        @block.tensor
        def _(h):
            for f in R.ops["tensor"]:
                f(h)

        @block.vector
        def _(h):
            for f in R.ops["vector"]:
                f(h)

        @block.scalar
        def _(h):
            for f in R.ops["scalar"]:
                f(h)

        @block.sync
        def _(h):
            for f in R.ops["sync"]:
                f(h)

        @block.gpsimd
        def _(h):
            for f in R.ops["gpsimd"]:
                f(h)

    nc._phase_tags = list(R.ttags)
    return nc


def _host_consts():
    t = np.arange(T, dtype=np.float64)
    inv = 1.0 / (10000.0 ** (np.arange(0, DH, 2, dtype=np.float64) / DH))
    fr = t[:, None] * inv[None, :]
    emb = np.concatenate([fr, fr], axis=1)
    cos = np.cos(emb).astype(np.float32).T
    sin = np.sin(emb).astype(np.float32).T
    cosb = np.concatenate([cos, cos], axis=0)
    sinb = np.concatenate([sin, sin], axis=0)
    blk = np.zeros((DH, DH), np.float32)
    for i in range(32):
        blk[i + 32, i] = -1.0
        blk[i, i + 32] = 1.0
    prot = np.zeros((P, P), np.float32)
    prot[:DH, :DH] = blk
    prot[DH:, DH:] = blk
    idf = np.eye(P, dtype=np.float32)
    return cosb, sinb, prot, idf


def _col128(v):
    v = np.asarray(v, dtype=np.float32).reshape(-1, P)
    return np.ascontiguousarray(v.T)


def kernel(**kw):
    cosb, sinb, prot, idf = _host_consts()
    base = {
        "w1": kw["L0_ffn_w1_w"], "w2": kw["L0_ffn_w2_w"],
        "h1": kw["L0_ffn_h1_w"], "h2": kw["L0_ffn_h2_w"], "w3": kw["L0_ffn_w3_w"],
        "s1c": _col128(kw["L0_ffn_w1_s"]), "s2c": _col128(kw["L0_ffn_w2_s"]),
        "sh1c": _col128(kw["L0_ffn_h1_s"]), "sh2c": _col128(kw["L0_ffn_h2_s"]),
        "sw3c": _col128(kw["L0_ffn_w3_s"]),
        "wdk": kw["L0_attn_w_dk"], "wdv": kw["L0_attn_w_dv"],
        "wuk": kw["L0_attn_w_uk"], "wuv": kw["L0_attn_w_uv"], "wo": kw["L0_attn_w_o"],
        "bdkc": _col128(kw["L0_attn_b_dk"]), "bdvc": _col128(kw["L0_attn_b_dv"]),
        "bukc": _col128(kw["L0_attn_b_uk"]),
        "buv_row": np.asarray(kw["L0_attn_b_uv"], np.float32).reshape(1, D),
        "boc": _col128(kw["L0_attn_b_o"]),
        "cosb": cosb, "sinb": sinb, "prot": prot, "idf": idf,
        "wq": kw["L1_attn_Wq"], "wk": kw["L1_attn_Wk"], "wv": kw["L1_attn_Wv"],
        "wproj": kw["L1_attn_Wproj"],
        "bqc": _col128(kw["L1_attn_bq"]), "bkc": _col128(kw["L1_attn_bk"]),
        "bv_row": np.asarray(kw["L1_attn_bv"], np.float32).reshape(1, D),
        "bprojc": _col128(kw["L1_attn_bproj"]),
        "wqf": kw["L1_attn_wq_feat"], "wkf": kw["L1_attn_wk_feat"],
        "l1w": kw["L1_ffn_weight"], "l1a1": kw["L1_ffn_aux_weight1"],
        "l1a2": kw["L1_ffn_aux_weight2"], "l1bc": _col128(kw["L1_ffn_bias"]),
    }
    base = {k: np.ascontiguousarray(np.asarray(v, dtype=np.float32)) for k, v in base.items()}
    x = np.asarray(kw["L0_attn_x"], dtype=np.float32)
    nc = build_nc(STEPS)
    in_maps = []
    for b in range(8):
        m = dict(base)
        m["x"] = np.ascontiguousarray(x[b])
        in_maps.append(m)
    res = run_bass_kernel_spmd(nc, in_maps, core_ids=list(range(8)))
    return np.stack([np.asarray(res.results[b]["out"]) for b in range(8)], axis=0)


# version 08 (final)
# speedup vs baseline: 1.1414x; 1.1414x over previous
import numpy as np
import concourse.bass as bass
import concourse.mybir as mybir
from concourse.alu_op_type import AluOpType
from concourse.bass_utils import run_bass_kernel_spmd

F32 = mybir.dt.float32
BF16 = mybir.dt.bfloat16
AF = mybir.ActivationFunctionType
AX = mybir.AxisListType
OP = AluOpType

P = 128
T = 2048
D = 1024
DC = D // P
TC = 512
NTC = T // TC
DH = 64
NH = 16
DHID = 4096
NCC = DHID // P
STEPS = 6

ENGINES = ("tensor", "vector", "scalar", "sync", "gpsimd")

WTOT = 106400
W_X = 0
W_R1 = 32768
W_R2 = 49152
W_R3 = 65536
W_R4 = 81920
W_R5 = 98560
W_PERM = 105200
W_TMP = W_R5
W_STAT = W_R5 + 4096
W_SQ = W_R5 + 4288
W_FEAT = W_R5 + 5312
W_XTR = W_R5 + 5824

XA_WDK, XA_WDV, XA_WUK, XA_WUV = 0, 1024, 2048, 3072
XA_COS, XA_SIN, XA_PROT, XA_BUV = 4096, 6144, 8192, 8320
XA_CK, XA_CV = 9344, 11392
XA_QEXT = 13440
XA_KEXT = 17536
XA_EXPT = 17792
XA_REC = 21888
XA_RECB = 22912
XA_KARR = 23936
XL_S0, XL_S1 = 0, 16384
XL_CTX, XL_REC, XL_RECB, XL_BV = 16384, 20544, 21568, 22592


class Sem:
    def __init__(self, nc, name):
        self.h = nc.alloc_semaphore(name=name)
        self.name = name
        self.n = 0

    def inc(self, k=1):
        self.n += k
        return self.n


_PSB = {"psA": 0, "psB": 4}


def _psum_banks(x):
    try:
        t = x.tensor
        nm = getattr(t, "name", None)
        if nm in _PSB:
            off = x.offset
            cols = max(x.free_nbytes() // 4, 1)
            return list(range(_PSB[nm] + off // 512,
                              _PSB[nm] + (off + cols - 1) // 512 + 1))
    except Exception:
        pass
    return None


class Rec:
    def __init__(self):
        self.ops = {e: [] for e in ENGINES}
        self.waited = {e: {} for e in ENGINES}
        self.phase = "init"
        self.ttags = []
        self.pbank = {}

    def E(self, eng, meth, *args, waits=(), inc=None, incn=1, **kw):
        waits = list(waits)
        if eng == "tensor" and meth in ("matmul", "transpose"):
            if meth == "transpose" or kw.get("start"):
                bl = _psum_banks(args[0])
                if bl:
                    for b in bl:
                        waits += self.pbank.get(b, [])
        elif eng in ("vector", "scalar"):
            bl = []
            for a in args:
                r = _psum_banks(a)
                if r:
                    bl += r
            if bl:
                assert inc is not None, f"psum consumer {eng}.{meth} needs inc"
                self._pb_pending = bl
        wl = []
        wd = self.waited[eng]
        for (s, v) in waits:
            if v is None or v == 0:
                continue
            assert v <= s.n, f"[{eng}.{meth}] wait {s.name}>={v} but recorded {s.n}"
            if wd.get(s.name, -1) < v:
                wd[s.name] = v
                wl.append((s.h, v))
        val = inc.inc(incn) if inc is not None else None
        ih = inc.h if inc is not None else None
        tag = self.phase

        def f(h, meth=meth, args=args, kw=kw, wl=tuple(wl), ih=ih, incn=incn, tag=tag):
            for (sh, v) in wl:
                h.wait_ge(sh, v)
            r = getattr(h, meth)(*args, **kw)
            try:
                r.annotate(tag)
            except Exception:
                pass
            if ih is not None:
                r.then_inc(ih, incn)
        self.ops[eng].append(f)
        bl = getattr(self, "_pb_pending", None)
        if bl is not None and inc is not None:
            for b in bl:
                self.pbank[b] = [(inc, val)]
            self._pb_pending = None
        return val

    def wait(self, eng, *pairs):
        wd = self.waited[eng]
        for (s, v) in pairs:
            if v is None or v == 0:
                continue
            assert v <= s.n
            if wd.get(s.name, -1) < v:
                wd[s.name] = v

                def f(h, sh=s.h, v=v):
                    h.wait_ge(sh, v)
                self.ops[eng].append(f)


def build_nc(steps=STEPS):
    nc = bass.Bass("TRN2", target_bir_lowering=False)
    ap = {}

    def din(name, shape):
        ap[name] = nc.dram_tensor(name, list(shape), F32, kind="ExternalInput").ap()

    for nm, sh in [("x", (T, D)),
                   ("w1", (DHID, D)), ("w2", (DHID, D)), ("h1", (DHID, DHID)),
                   ("h2", (DHID, DHID)), ("w3", (D, DHID)),
                   ("s1c", (P, NCC)), ("s2c", (P, NCC)), ("sh1c", (P, NCC)),
                   ("sh2c", (P, NCC)), ("sw3c", (P, DC)),
                   ("wdk", (D, P)), ("wdv", (D, P)), ("wuk", (P, D)),
                   ("wuv", (P, D)), ("wo", (D, D)),
                   ("bdkc", (P, 1)), ("bdvc", (P, 1)), ("bukc", (P, DC)),
                   ("buv_row", (1, D)), ("boc", (P, DC)),
                   ("cosb", (P, T)), ("sinb", (P, T)), ("prot", (P, P)),
                   ("idf", (P, P)),
                   ("wq", (D, D)), ("wk", (D, D)), ("wv", (D, D)), ("wproj", (D, D)),
                   ("bqc", (P, DC)), ("bkc", (P, DC)), ("bv_row", (1, D)),
                   ("bprojc", (P, DC)), ("wqf", (DH, 256)), ("wkf", (DH, 256)),
                   ("l1w", (D, D)), ("l1a1", (D, D)), ("l1a2", (D, D)),
                   ("l1bc", (P, DC))]:
        din(nm, sh)
    out = nc.dram_tensor("out", [T, D], F32, kind="ExternalOutput").ap()

    w1bf = nc.dram_tensor("w1bf", [DHID, D], BF16, kind="Internal").ap()
    w2bf = nc.dram_tensor("w2bf", [DHID, D], BF16, kind="Internal").ap()
    h1bf = nc.dram_tensor("h1bf", [DHID, DHID], BF16, kind="Internal").ap()
    h2bf = nc.dram_tensor("h2bf", [DHID, DHID], BF16, kind="Internal").ap()
    w3bf = nc.dram_tensor("w3bf", [D, DHID], BF16, kind="Internal").ap()
    xsp = nc.dram_tensor("xsp", [P, DC * T], F32, kind="Internal").ap()
    wcbf = nc.dram_tensor("wcbf", [P, DC * D], BF16, kind="Internal").ap()

    with (
        nc.sbuf_tensor("SB", [P, WTOT], BF16) as SBt,
        nc.psum_tensor("psA", [P, 2048], F32) as psA,
        nc.psum_tensor("psB", [P, 2048], F32) as psB,
        nc.Block() as block,
    ):
        SBf = SBt[:, :].bitcast(F32)

        def bw(off, n, p0=0, p1=P):
            return SBt[p0:p1, off:off + n]

        def fv(off, nf, p0=0, p1=P):
            assert off % 2 == 0
            return SBf[p0:p1, off // 2:off // 2 + nf]

        R = Rec()
        sT, sV, sS = Sem(nc, "sT"), Sem(nc, "sV"), Sem(nc, "sS")
        sQ, sG = Sem(nc, "sQ"), Sem(nc, "sG")

        lastuse = {}

        def reg_dep(name):
            return list(lastuse.get(name, lastuse.get("__bar__", [])))

        def reg_use(name, sem, val):
            lastuse[name] = [(sem, val)]

        def barrier():
            snap = [(sT, sT.n), (sV, sV.n), (sS, sS.n)]
            for e in ("tensor", "vector", "scalar"):
                R.wait(e, *snap)
            for k in list(lastuse.keys()):
                lastuse[k] = list(snap)
            lastuse["__bar__"] = list(snap)
            for b in list(R.pbank.keys()):
                R.pbank[b] = list(snap)

        def pA(i):
            return psA[:, i * 512:(i + 1) * 512]

        def pB(i):
            return psB[:, i * 512:(i + 1) * 512]

        def xv(ic, t0, n):
            return SBf[:, ic * T + t0: ic * T + t0 + n]

        PO = {}
        _next = [W_PERM]
        PO["idbf"] = W_XTR + 512
        for nm, wds in [("allones", P), ("lnones", P), ("onesf", 2 * P),
                        ("idf", 2 * P), ("s1", 2 * NCC), ("s2", 2 * NCC),
                        ("sh1", 2 * NCC), ("sh2", 2 * NCC), ("sw3h", 2 * DC),
                        ("bdk", 2), ("bdv", 2), ("buk", 2 * DC), ("bo", 2 * DC),
                        ("bq", 2 * DC), ("bk", 2 * DC), ("bproj", 2 * DC),
                        ("l1b", 2 * DC), ("c_eps5", 2), ("c_eps6", 2)]:
            off = _next[0]
            assert off + wds <= WTOT
            _next[0] += wds
            PO[nm] = off

        def scol(name, j):
            return SBf[:, PO[name] // 2 + j: PO[name] // 2 + j + 1]

        g_id = R.E("gpsimd", "dma_start", fv(PO["idf"], P), ap["idf"][:, :], inc=sG, incn=16)
        R.E("gpsimd", "dma_start", bw(PO["idbf"], P), ap["idf"][:, :], inc=sG, incn=16)
        for nm, src, n in [("s1", "s1c", NCC), ("s2", "s2c", NCC), ("sh1", "sh1c", NCC),
                           ("sh2", "sh2c", NCC), ("buk", "bukc", DC), ("bo", "boc", DC),
                           ("bdk", "bdkc", 1), ("bdv", "bdvc", 1), ("bq", "bqc", DC),
                           ("bk", "bkc", DC), ("bproj", "bprojc", DC), ("l1b", "l1bc", DC)]:
            R.E("gpsimd", "dma_start", fv(PO[nm], n), ap[src][:, :], inc=sG, incn=16)
        g_sw3 = R.E("gpsimd", "dma_start", fv(PO["sw3h"], DC), ap["sw3c"][:, :], inc=sG, incn=16)
        v_sw3 = R.E("vector", "tensor_scalar", fv(PO["sw3h"], DC), fv(PO["sw3h"], DC),
                    0.5, None, OP.mult, waits=[(sG, g_sw3)], inc=sV)
        R.E("vector", "memset", bw(PO["allones"], P), 1.0, inc=sV)
        R.E("vector", "memset", fv(PO["onesf"], P), 1.0, inc=sV)
        v_ones = R.E("vector", "memset", bw(PO["lnones"], P), 1.0 / 1024.0, inc=sV)
        R.E("vector", "memset", fv(PO["c_eps5"], 1), 1e-5, inc=sV)
        R.E("vector", "memset", fv(PO["c_eps6"], 1), 1e-6, inc=sV)
        nc.const_aps.register(F32, 1e-5, fv(PO["c_eps5"], 1))
        nc.const_aps.register(F32, 1e-6, fv(PO["c_eps6"], 1))

        def cast2d(dst, src, rows, nsplit):
            per = rows // nsplit
            v = None
            for i in range(nsplit):
                v = R.E("gpsimd", "dma_start", dst[i * per:(i + 1) * per, :],
                        src[i * per:(i + 1) * per, :], inc=sG, incn=16)
            return v

        def cast2d_parts(dst, src, rows, nsplit):
            per = rows // nsplit
            vs = []
            for i in range(nsplit):
                vs.append(R.E("gpsimd", "dma_start", dst[i * per:(i + 1) * per, :],
                              src[i * per:(i + 1) * per, :], inc=sG, incn=16))
            return vs

        g_w1p = cast2d_parts(w1bf, ap["w1"], DHID, 4)
        g_w1 = g_w1p[-1]
        g_w2p = cast2d_parts(w2bf, ap["w2"], DHID, 4)
        g_w2 = g_w2p[-1]
        g_h1p = cast2d_parts(h1bf, ap["h1"], DHID, 16)
        g_h1 = g_h1p[-1]
        g_h2p = cast2d_parts(h2bf, ap["h2"], DHID, 16)
        g_h2 = g_h2p[-1]
        g_w3 = cast2d(w3bf, ap["w3"], D, 4)

        R.phase = "xload"
        stg = [fv(W_TMP, D), fv(W_TMP + 2048, D)]
        ev_last = {}
        for tt in range(16):
            sl = tt % 2
            q = R.E("sync", "dma_start", stg[sl], ap["x"][tt * P:(tt + 1) * P, :],
                    waits=reg_dep(f"stg{sl}"), inc=sQ, incn=16)
            for ic in range(DC):
                bk = (tt * DC + ic) % 4
                tv = R.E("tensor", "transpose", pA(bk)[:, 0:P],
                         stg[sl][:, ic * P:(ic + 1) * P], fv(PO["idf"], P),
                         waits=[(sQ, q), (sG, g_id)] +
                         ([(sV, ev_last[bk])] if bk in ev_last else []),
                         inc=sT)
                ev_last[bk] = R.E("vector", "tensor_copy", xv(ic, tt * P, P),
                                  pA(bk)[:, 0:P], waits=[(sT, tv)], inc=sV)
            reg_use(f"stg{sl}", sT, sT.n)
        barrier()

        tmpM = fv(W_TMP, TC)
        tmpV_ = fv(W_TMP + 1024, TC)
        tmpR = fv(W_TMP + 2048, TC)
        tmpD = fv(W_TMP + 3072, TC)

        def ln_tc(tc, dst_fn):
            R.phase = "ln"
            t0 = tc * TC
            dep = reg_dep("tmp") + reg_dep("lnps")
            cast_done = []
            for ic in range(DC):
                c = R.E("vector", "tensor_copy", dst_fn(ic), xv(ic, t0, TC),
                        waits=(dep if ic == 0 else []), inc=sV)
                cast_done.append(c)
            for ic in range(DC):
                R.E("tensor", "matmul", pA(0), bw(PO["lnones"], P), dst_fn(ic),
                    start=(ic == 0), stop=(ic == DC - 1),
                    waits=[(sV, cast_done[ic]), (sV, v_ones)] + (dep if ic == 0 else []),
                    inc=sT if ic == DC - 1 else None)
            t_mean = sT.n
            sq_mm = {}
            for ic in range(DC):
                sqo = W_SQ if ic % 2 == 0 else W_SQ + TC
                w = ([(sT, sq_mm[ic - 2])] if ic >= 2 else []) + \
                    (reg_dep("sq") if ic < 2 else [])
                sv = R.E("scalar", "activation", bw(sqo, TC), xv(ic, t0, TC), AF.Square,
                         waits=w + (dep if ic == 0 else []), inc=sS)
                sq_mm[ic] = R.E("tensor", "matmul", pA(1), bw(PO["lnones"], P), bw(sqo, TC),
                                start=(ic == 0), stop=(ic == DC - 1),
                                waits=[(sS, sv)], inc=sT)
            t_sq = sT.n
            reg_use("sq", sT, t_sq)
            sm = R.E("scalar", "copy", tmpM, pA(0), waits=[(sT, t_mean)], inc=sS)
            v1 = R.E("vector", "tensor_mul", tmpV_, tmpM, tmpM, waits=[(sS, sm)], inc=sV)
            v2 = R.E("vector", "tensor_sub", tmpV_, pA(1), tmpV_,
                     waits=[(sT, t_sq), (sV, v1)], inc=sV)
            s1_ = R.E("scalar", "activation", tmpR, tmpV_, AF.Ln, bias=1e-5,
                      waits=[(sV, v2)], inc=sS)
            s2_ = R.E("scalar", "activation", tmpR, tmpR, AF.Exp, scale=-0.5,
                      waits=[(sS, s1_)], inc=sS)
            last = None
            for ic in range(DC):
                d1 = R.E("vector", "tensor_sub", tmpD, xv(ic, t0, TC), tmpM,
                         waits=[(sS, s2_)], inc=sV)
                last = R.E("vector", "tensor_mul", dst_fn(ic), tmpD, tmpR,
                           waits=[(sV, d1)], inc=sV)
            reg_use("tmp", sV, last)
            reg_use("lnps", sV, last)
            return last

        def ffn0_app():
            OFF_XN, OFF_H, OFF_G = W_R1, W_R2, W_R3
            SLAB = [W_R4 + i * 4096 for i in range(4)]

            def tr3(off, n):
                return bw(off, n).rearrange("p (m r) -> p m r", r=P)

            for tc in range(NTC):
                ln_done = ln_tc(tc, lambda ic: bw(OFF_XN + ic * TC, TC))

                R.phase = "f0_w12"
                w1d, w2d = {}, {}

                def ld_w12(oc):
                    d1 = reg_dep(f"fs0_{oc % 4}")
                    w1d[oc] = R.E("sync", "dma_start", tr3(SLAB[0] + (oc % 4) * 1024, 1024),
                                  w1bf[oc * P:(oc + 1) * P, :], transpose=True,
                                  waits=d1 + [(sG, g_w1p[oc // 8])], inc=sQ, incn=16)
                    d2 = reg_dep(f"fs1_{oc % 4}")
                    w2d[oc] = R.E("sync", "dma_start", tr3(SLAB[1] + (oc % 4) * 1024, 1024),
                                  w2bf[oc * P:(oc + 1) * P, :], transpose=True,
                                  waits=d2 + [(sG, g_w2p[oc // 8])], inc=sQ, incn=16)

                for oc in range(3):
                    ld_w12(oc)
                sil = fv(W_TMP, TC)
                for oc in range(NCC):
                    if oc + 3 < NCC:
                        ld_w12(oc + 3)
                    ba = oc % 2
                    pa, pb_ = pA(ba), pA(2 + ba)
                    dep_a = reg_dep(f"fpa{ba}")
                    dep_b = reg_dep(f"fpb{ba}")
                    for ic in range(DC):
                        R.E("tensor", "matmul", pa,
                            bw(SLAB[0] + (oc % 4) * 1024 + ic * P, P),
                            bw(OFF_XN + ic * TC, TC),
                            start=(ic == 0), stop=(ic == DC - 1),
                            waits=([(sQ, w1d[oc]), (sV, ln_done)] + dep_a)
                            if ic == 0 else [],
                            inc=sT if ic == DC - 1 else None)
                    ta = sT.n
                    for ic in range(DC):
                        R.E("tensor", "matmul", pb_,
                            bw(SLAB[1] + (oc % 4) * 1024 + ic * P, P),
                            bw(OFF_XN + ic * TC, TC),
                            start=(ic == 0), stop=(ic == DC - 1),
                            waits=([(sQ, w2d[oc])] + dep_b) if ic == 0 else [],
                            inc=sT if ic == DC - 1 else None)
                    tb = sT.n
                    reg_use(f"fs0_{oc % 4}", sT, ta)
                    reg_use(f"fs1_{oc % 4}", sT, tb)
                    ss = R.E("scalar", "activation", sil, pa, AF.Silu,
                             scale=scol("s1", oc),
                             waits=[(sT, ta)] + (reg_dep("tmp") if oc == 0 else []),
                             inc=sS)
                    hv = R.E("vector", "scalar_tensor_tensor", bw(OFF_H + oc * TC, TC),
                             pb_, scol("s2", oc), sil, OP.mult, OP.mult,
                             waits=[(sT, tb), (sS, ss)], inc=sV)
                    reg_use(f"fpa{ba}", sS, ss)
                    reg_use(f"fpb{ba}", sV, hv)
                reg_use("tmp", sV, sV.n)
                h_done = sV.n

                def big_phase(wdram, g_cast, OFF_IN, OFF_OUT, scname, n_oc, act,
                              tag, in_sem, in_val):
                    bd = {}

                    def ld(oc):
                        d = reg_dep(f"{tag}_{oc % 4}")
                        gv = g_cast[oc // 2] if isinstance(g_cast, list) else g_cast
                        bd[oc] = R.E("sync", "dma_start", tr3(SLAB[oc % 4], 4096),
                                     wdram[oc * P:(oc + 1) * P, :], transpose=True,
                                     waits=d + [(sG, gv)], inc=sQ, incn=16)

                    for oc in range(min(3, n_oc)):
                        ld(oc)
                    for op in range(n_oc // 2):
                        for j in range(2):
                            oc = op * 2 + j
                            if oc + 3 < n_oc:
                                ld(oc + 3)
                            pj = pA(j) if op % 2 == 0 else pA(2 + j)
                            dep_ps = reg_dep(f"{tag}p{j}{op % 2}")
                            for cc in range(NCC):
                                R.E("tensor", "matmul", pj,
                                    bw(SLAB[oc % 4] + cc * P, P),
                                    bw(OFF_IN + cc * TC, TC),
                                    start=(cc == 0), stop=(cc == NCC - 1),
                                    waits=([(sQ, bd[oc]), (in_sem, in_val)] + dep_ps)
                                    if cc == 0 else [],
                                    inc=sT if cc == NCC - 1 else None)
                            tj = sT.n
                            reg_use(f"{tag}_{oc % 4}", sT, tj)
                            if act == "gelu":
                                ev = R.E("scalar", "activation", bw(OFF_OUT + oc * TC, TC),
                                         pj, AF.Gelu, scale=scol(scname, oc),
                                         waits=[(sT, tj)], inc=sS)
                                reg_use(f"{tag}p{j}{op % 2}", sS, ev)
                            else:
                                ev = R.E("vector", "scalar_tensor_tensor",
                                         xv(oc, tc * TC, TC), pj, scol("sw3h", oc),
                                         xv(oc, tc * TC, TC), OP.mult, OP.add,
                                         waits=[(sT, tj), (sV, v_sw3)], inc=sV)
                                reg_use(f"{tag}p{j}{op % 2}", sV, ev)

                R.phase = "f0_h1"
                big_phase(h1bf, g_h1p, OFF_H, OFF_G, "sh1", NCC, "gelu", "fh1", sV, h_done)
                g1_done = sS.n
                R.phase = "f0_h2"
                big_phase(h2bf, g_h2p, OFF_G, OFF_H, "sh2", NCC, "gelu", "fh2", sS, g1_done)
                g2_done = sS.n
                R.phase = "f0_w3"
                big_phase(w3bf, g_w3, OFF_H, None, None, DC, "xadd", "fw3", sS, g2_done)

        def l0_attn():
            Q_, K_ = W_R1, W_R2
            ln_last = None
            for tc in range(NTC):
                ln_last = ln_tc(tc, lambda ic, tc=tc: bw(Q_ + ic * T + tc * TC, TC))
            spl = R.E("sync", "dma_start", xsp[:, :], SBf[:, 0:DC * T],
                      waits=[(sV, ln_last), (sS, sS.n)], inc=sQ, incn=16)
            dep = [(sQ, spl)]
            R.E("gpsimd", "dma_start", bw(XA_WDK, 1024),
                ap["wdk"].rearrange("(ic p) c -> p ic c", p=P), waits=dep, inc=sG, incn=16)
            R.E("gpsimd", "dma_start", bw(XA_WDV, 1024),
                ap["wdv"].rearrange("(ic p) c -> p ic c", p=P), inc=sG, incn=16)
            R.E("gpsimd", "dma_start", bw(XA_WUK, 1024), ap["wuk"][:, :], inc=sG, incn=16)
            R.E("gpsimd", "dma_start", bw(XA_WUV, 1024), ap["wuv"][:, :], inc=sG, incn=16)
            R.E("gpsimd", "dma_start", bw(XA_COS, T), ap["cosb"][:, :], inc=sG, incn=16)
            R.E("gpsimd", "dma_start", bw(XA_SIN, T), ap["sinb"][:, :], inc=sG, incn=16)
            R.E("gpsimd", "dma_start", bw(XA_PROT, P), ap["prot"][:, :], inc=sG, incn=16)
            g_sm = R.E("gpsimd", "dma_start", bw(XA_BUV, D, 0, 1), ap["buv_row"][:, :],
                       inc=sG, incn=16)
            z1 = R.E("vector", "memset", bw(XA_QEXT, 2 * T), 0.0, waits=dep, inc=sV)

            vz = None
            for tt in range(16):
                onescol = bw(W_R4 + tt * 1040 + 64, 8 * 129).rearrange(
                    "p (h d) -> p h d", d=129)[:, :, 0:1]
                vz = R.E("vector", "memset", onescol, 1.0,
                         waits=reg_dep("vbufz") if tt == 0 else [], inc=sV)

            R.phase = "a0_ckv"
            for (dst, wofs, bname) in ((XA_CK, XA_WDK, "bdk"), (XA_CV, XA_WDV, "bdv")):
                for tc in range(NTC):
                    pj = pA(tc % 2)
                    depp = reg_dep(f"ckps{tc % 2}")
                    for ic in range(DC):
                        R.E("tensor", "matmul", pj, bw(wofs + ic * P, P),
                            bw(Q_ + ic * T + tc * TC, TC),
                            start=(ic == 0), stop=(ic == DC - 1),
                            waits=([(sG, g_sm), (sV, ln_last)] + depp) if ic == 0 else [],
                            inc=sT if ic == DC - 1 else None)
                    ev = R.E("scalar", "activation", bw(dst + tc * TC, TC), pj, AF.Identity,
                             bias=scol(bname, 0), waits=[(sT, sT.n)], inc=sS)
                    reg_use(f"ckps{tc % 2}", sS, ev)

            R.phase = "a0_kT"
            ck_done = sS.n
            for oc in range(DC):
                for tc in range(NTC):
                    pj = pA(2 + (tc % 2))
                    depp = reg_dep(f"kps{tc % 2}")
                    R.E("tensor", "matmul", pj, bw(XA_WUK + oc * P, P),
                        bw(XA_CK + tc * TC, TC), start=True, stop=True,
                        waits=[(sS, ck_done)] + depp, inc=sT)
                    ev = R.E("scalar", "activation", bw(K_ + oc * T + tc * TC, TC), pj,
                             AF.Identity, bias=scol("buk", oc), waits=[(sT, sT.n)], inc=sS)
                    reg_use(f"kps{tc % 2}", sS, ev)
            kT_done = sS.n

            R.phase = "a0_v"
            for tt in range(16):
                for og in range(2):
                    pj = pB(og)
                    depp = reg_dep(f"vps{og}")
                    R.E("tensor", "matmul", pj, bw(XA_CV + tt * P, P),
                        bw(XA_WUV + og * TC, TC), start=True, stop=False,
                        waits=([(sS, ck_done)] + depp), inc=None)
                    R.E("tensor", "matmul", pj, bw(PO["allones"], P, 0, 1),
                        bw(XA_BUV + og * TC, TC, 0, 1), start=False, stop=True,
                        waits=[(sG, g_sm)], inc=sT)
                    tj = sT.n
                    base = W_R4 + tt * 1040 + og * 4 * 129
                    pview = psB[:, og * 512: og * 512 + 512].rearrange(
                        "p (h e d) -> p h e d", h=4, e=2, d=64)
                    dve = bw(base, 4 * 129).rearrange("p (h d) -> p h d", d=129)[:, :, 0:64]
                    dvo = bw(base + 65, 4 * 129).rearrange(
                        "p (h d) -> p h d", d=129)[:, :, 0:64]
                    R.E("vector", "tensor_copy", dve, pview[:, :, 0, :],
                        waits=[(sT, tj), (sV, vz)], inc=sV)
                    ev2 = R.E("vector", "tensor_copy", dvo, pview[:, :, 1, :],
                              waits=[(sT, tj)], inc=sV)
                    reg_use(f"vps{og}", sV, ev2)
            v_done = sV.n
            t_after_v = sT.n

            R.phase = "a0_rope"
            rt1 = fv(W_TMP, TC)
            rt2 = fv(W_TMP + 1024, TC)
            rope_start = [(sT, t_after_v), (sS, kT_done), (sV, v_done)] + reg_dep("tmp")
            last_rope = None
            for (buf, _) in ((Q_, "q"), (K_, "k")):
                for fc in range(DC):
                    for tc in range(NTC):
                        pj = pB(2 + (tc % 2))
                        depp = reg_dep(f"rps{tc % 2}")
                        tv = R.E("tensor", "matmul", pj, bw(XA_PROT, P),
                                 bw(buf + fc * T + tc * TC, TC), start=True, stop=True,
                                 waits=[(sG, g_sm)] + depp +
                                 (rope_start if last_rope is None else []),
                                 inc=sT)
                        a1 = R.E("vector", "tensor_mul", rt1, bw(buf + fc * T + tc * TC, TC),
                                 bw(XA_COS + tc * TC, TC),
                                 waits=([(sG, g_sm)] + (rope_start if last_rope is None else [])),
                                 inc=sV)
                        a2 = R.E("vector", "tensor_mul", rt2, pj, bw(XA_SIN + tc * TC, TC),
                                 waits=[(sT, tv), (sV, a1)], inc=sV)
                        last_rope = R.E("vector", "tensor_add",
                                        bw(buf + fc * T + tc * TC, TC), rt1, rt2,
                                        waits=[(sV, a2)], inc=sV)
                        reg_use(f"rps{tc % 2}", sV, last_rope)
            reg_use("tmp", sV, last_rope)
            rope_done = last_rope

            statf = fv(W_STAT, 96)
            stats_done = {}
            ex_hold = [None, None]

            def p1_subunits(h):
                fc, par = h // 2, h % 2
                p0 = par * 64
                mcol = 64 if par == 0 else 32
                QEXT = XA_QEXT + par * T
                j = 0
                for t1t in range(16):
                    for t2q in range(4):
                        bk = 2 + (j % 2)
                        dep = reg_dep(f"p1b{j % 2}") + ([(sV, rope_done)]
                                                        if h == 0 and j == 0 else [])
                        mmv = R.E("tensor", "matmul",
                                  psA[:, bk * 512:(bk + 1) * 512],
                                  bw(Q_ + fc * T + t1t * P, P, p0, p0 + 64),
                                  bw(K_ + fc * T + t2q * 512, 512, p0, p0 + 64),
                                  start=True, stop=True, waits=dep, inc=sT)
                        red = R.E("vector", "tensor_reduce", statf[:, 70 + t2q:71 + t2q],
                                  psA[:, bk * 512:(bk + 1) * 512], AX.X, OP.max,
                                  negate=True, waits=[(sT, mmv)], inc=sV)
                        reg_use(f"p1b{j % 2}", sV, red)
                        if t2q == 3:
                            R.E("vector", "tensor_tensor", statf[:, 74:75],
                                statf[:, 70:71], statf[:, 71:72], OP.min,
                                waits=[(sV, red)], inc=sV)
                            R.E("vector", "tensor_tensor", statf[:, 75:76],
                                statf[:, 72:73], statf[:, 73:74], OP.min, inc=sV)
                            cm = R.E("vector", "tensor_tensor", statf[:, mcol:mcol + 1],
                                     statf[:, 74:75], statf[:, 75:76], OP.min, inc=sV)
                            tr = R.E("tensor", "transpose", psA[0:65, 1024:1024 + P],
                                     statf[:, 0:65], fv(PO["idf"], P),
                                     waits=[(sV, cm)] + reg_dep("stps") +
                                     reg_dep("p1b0"), inc=sT)
                            reg_use("p1b0", sT, tr)
                            cp = R.E("scalar", "copy",
                                     bw(QEXT + t1t * P, P, mcol, mcol + 1),
                                     psA[mcol:mcol + 1, 1024:1024 + P],
                                     waits=[(sT, tr)], inc=sS)
                            reg_use("stps", sS, cp)
                            reg_use("p1b0", sS, cp)
                        j += 1
                        yield

            R.phase = "a0_pass1"
            for _ in p1_subunits(0):
                pass
            stats_done[0] = sS.n

            for h in range(NH):
                fc, par = h // 2, h % 2
                p0 = par * 64
                mcol = 64 if par == 0 else 32
                QEXT = XA_QEXT + par * T
                hp = h // 2
                gen = p1_subunits(h + 1) if h + 1 < NH else None
                qc = R.E("scalar", "copy", bw(QEXT, T, p0, p0 + 64),
                         bw(Q_ + fc * T, T, p0, p0 + 64),
                         waits=[(sV, z1), (sS, stats_done[h])], inc=sS)
                kz = R.E("vector", "memset", bw(XA_KARR, 2048), 0.0,
                         waits=reg_dep("karr"), inc=sV)
                R.E("vector", "memset", bw(XA_KARR, 2048, mcol, mcol + 1), 1.0,
                    inc=sV)
                kcp = R.E("vector", "tensor_copy", bw(XA_KARR, 2048, p0, p0 + 64),
                          bw(K_ + fc * T, T, p0, p0 + 64),
                          waits=[(sV, rope_done)], inc=sV)
                R.phase = "a0_pass2"
                for t1h in range(2):
                    avb = (0 if t1h == 0 else 2)
                    dnb = (2 if t1h == 0 else 0)
                    av_dep = reg_dep(f"avps{avb}") + reg_dep(f"avps{avb + 1}")
                    dn_dep = reg_dep(f"avps{dnb}") + reg_dep(f"avps{dnb + 1}")
                    av_done = {}
                    pend = None

                    def emit_av(pd, avb=avb, dnb=dnb, av_dep=av_dep, dn_dep=dn_dep,
                                hp=hp, par=par):
                        (i, slot, exs) = pd
                        blk = W_R4 + i * 1040 + hp * 129
                        if par == 0:
                            lhs = SBt[:, blk: blk + 65]

                            def avout(n):
                                return psB[0:65, (avb + n) * 512:(avb + n + 1) * 512]
                        else:
                            lhs = SBt[:, blk + 1: blk + 129]

                            def avout(n):
                                return psB[0:128, (avb + n) * 512:(avb + n + 1) * 512]
                        last = None
                        for n in range(2):
                            last = R.E("tensor", "matmul", avout(n), lhs,
                                       bw(slot + n * 512, 512),
                                       start=(i == 0), stop=(i == 15),
                                       waits=([(sS, exs[1]), (sV, v_done)] +
                                              (av_dep if i == 0 else []))
                                       if n == 0 else [],
                                       inc=sT if (par == 0 and n == 1) else None)
                        if par == 1:
                            for n in range(2):
                                last = R.E("tensor", "matmul",
                                           psB[0:1, (dnb + n) * 512:(dnb + n + 1) * 512],
                                           bw(PO["allones"], 1), bw(slot + n * 512, 512),
                                           start=(i == 0), stop=(i == 15),
                                           waits=(dn_dep if i == 0 and n == 0 else []),
                                           inc=sT if n == 1 else None)
                        av_done[i] = last

                    for t2t in range(16):
                        slot = XA_EXPT + (t2t % 2) * 1024
                        mmv = [None, None]
                        for n in range(2):
                            mmv[n] = R.E("tensor", "matmul",
                                         psA[:, n * 512:(n + 1) * 512],
                                         bw(XA_KARR + t2t * P, P),
                                         bw(QEXT + t1h * 1024 + n * 512, 512),
                                         start=True, stop=True,
                                         waits=([(sV, kcp), (sS, qc)] if t2t == 0 and n == 0
                                                else []) +
                                         ([(sS, ex_hold[n])] if ex_hold[n] else []),
                                         inc=sT)
                        if pend is not None:
                            emit_av(pend)
                        exs = [None, None]
                        for n in range(2):
                            exs[n] = R.E("scalar", "activation", bw(slot + n * 512, 512),
                                         psA[:, n * 512:(n + 1) * 512], AF.Exp, scale=0.125,
                                         waits=[(sT, mmv[n])] +
                                         ([(sT, av_done[t2t - 2])] if t2t >= 2 else []),
                                         inc=sS)
                        ex_hold[0] = exs[0]
                        ex_hold[1] = exs[1]
                        pend = (t2t, slot, exs)
                        if gen is not None:
                            next(gen, None)
                            next(gen, None)
                    emit_av(pend)
                    t_av = sT.n
                    reg_use("karr", sT, t_av)
                    R.phase = "a0_norm"
                    for n in range(2):
                        if par == 0:
                            drow, dslice = 64, psB[64:65, (avb + n) * 512:(avb + n + 1) * 512]
                        else:
                            drow, dslice = 0, psB[0:1, (dnb + n) * 512:(dnb + n + 1) * 512]
                        rc = R.E("vector", "reciprocal",
                                 fv(XA_REC, 512, drow, drow + 1), dslice,
                                 waits=[(sT, t_av)] + reg_dep("recrow"), inc=sV)
                        bc = R.E("tensor", "matmul", psB[:, (dnb + n) * 512:(dnb + n + 1) * 512],
                                 fv(PO["onesf"], P, drow, drow + 1),
                                 fv(XA_REC, 512, drow, drow + 1), start=True, stop=True,
                                 waits=[(sV, rc)] + reg_dep(f"avps{dnb + n}"), inc=sT)
                        cb = R.E("scalar", "copy", fv(XA_RECB, 512),
                                 psB[:, (dnb + n) * 512:(dnb + n + 1) * 512],
                                 waits=[(sT, bc)] + reg_dep("recbb"), inc=sS)
                        ao = R.E("vector", "tensor_mul",
                                 bw(Q_ + fc * T + t1h * 1024 + n * 512, 512, p0, p0 + 64),
                                 psB[p0:p0 + 64, (avb + n) * 512:(avb + n + 1) * 512],
                                 fv(XA_RECB, 512, p0, p0 + 64),
                                 waits=[(sS, cb)], inc=sV)
                        reg_use("recrow", sV, ao)
                        reg_use("recbb", sV, ao)
                        reg_use(f"avps{avb + n}", sV, ao)
                        reg_use(f"avps{dnb + n}", sS, cb)
                    R.phase = "a0_pass2"
                if gen is not None:
                    for _ in gen:
                        pass
                    stats_done[h + 1] = sS.n
            aout_done = sV.n
            t_vbuf_free = sT.n

            xr = R.E("sync", "dma_start", SBf[:, 0:DC * T], xsp[:, :],
                     waits=[(sV, aout_done), (sS, sS.n), (sT, sT.n)], inc=sQ, incn=16)
            g_wo = R.E("gpsimd", "dma_start", bw(W_R2, DC * D),
                       ap["wo"].rearrange("(ic p) o -> p ic o", p=P),
                       waits=[(sT, sT.n), (sV, aout_done)], inc=sG, incn=16)
            R.phase = "a0_wo"
            for tc in range(NTC):
                for oc in range(DC):
                    k = (tc * DC + oc) % 2
                    pj = pA(k)
                    depp = reg_dep(f"wops{k}")
                    for ic in range(DC):
                        R.E("tensor", "matmul", pj, bw(W_R2 + ic * D + oc * P, P),
                            bw(Q_ + ic * T + tc * TC, TC),
                            start=(ic == 0), stop=(ic == DC - 1),
                            waits=([(sG, g_wo), (sV, aout_done), (sQ, xr)] + depp)
                            if ic == 0 else [], inc=sT if ic == DC - 1 else None)
                    tj = sT.n
                    ev = R.E("vector", "scalar_tensor_tensor", xv(oc, tc * TC, TC),
                             pj, scol("bo", oc), xv(oc, tc * TC, TC), OP.add, OP.add,
                             waits=[(sT, tj)], inc=sV)
                    reg_use(f"wops{k}", sV, ev)
            return t_vbuf_free

        def l1_ffn(first):
            OFF_XN = W_R1
            SLW = [W_R2, W_R3, W_R4]
            WTB = W_R1 + 4096
            M1B = W_R4 + 8192
            if first:
                gw = []
                for i, nm in enumerate(("l1w", "l1a1", "l1a2")):
                    g = R.E("gpsimd", "dma_start", bw(SLW[i], DC * D),
                            ap[nm].rearrange("(ic p) o -> p ic o", p=P),
                            waits=reg_dep(f"l1w{i}"), inc=sG, incn=16)
                    gw.append(g)
                tz = {}
                for ic in range(DC):
                    for mc in range(DC):
                        bk = (ic * DC + mc) % 4
                        tv = R.E("tensor", "transpose",
                                 pA(bk).bitcast(BF16)[:, 0:P],
                                 bw(SLW[0] + ic * D + mc * P, P),
                                 bw(PO["idbf"], P),
                                 waits=([(sG, gw[0])] if ic == 0 and mc == 0 else []),
                                 inc=sT)
                        tz[bk] = R.E("vector", "tensor_copy",
                                     bw(WTB + mc * D + ic * P, P),
                                     pA(bk).bitcast(BF16)[:, 0:P],
                                     waits=[(sT, tv)], inc=sV)
                wt_done = sV.n
                for oc in range(DC):
                    for nv in range(2):
                        bk = (oc * 2 + nv) % 4
                        for mc in range(DC):
                            R.E("tensor", "matmul", pA(bk),
                                bw(SLW[1] + mc * D + oc * P, P),
                                bw(WTB + mc * D + nv * 512, 512),
                                start=(mc == 0), stop=(mc == DC - 1),
                                waits=[(sG, gw[1]), (sV, wt_done)] if mc == 0 else [],
                                inc=sT if mc == DC - 1 else None)
                        R.E("vector", "tensor_copy",
                            bw(M1B + oc * D + nv * 512, 512), pA(bk),
                            waits=[(sT, sT.n)], inc=sV)
                m1_done = sV.n
                wc_ev = None
                for ib in range(DC):
                    for nv in range(2):
                        bk = (ib * 2 + nv) % 4
                        for oc in range(DC):
                            R.E("tensor", "matmul", pA(bk),
                                bw(M1B + oc * D + ib * P, P),
                                bw(SLW[2] + oc * D + nv * 512, 512),
                                start=(oc == 0), stop=(oc == DC - 1),
                                waits=[(sG, gw[2]), (sV, m1_done)] if oc == 0 else [],
                                inc=sT if oc == DC - 1 else None)
                        wc_ev = R.E("vector", "tensor_copy",
                                    bw(SLW[0] + ib * D + nv * 512, 512), pA(bk),
                                    waits=[(sT, sT.n)], inc=sV)
                wq_st = R.E("sync", "dma_start", wcbf[:, :], bw(SLW[0], DC * D),
                            waits=[(sV, wc_ev)], inc=sQ, incn=16)
                wc_ready = (sV, wc_ev)
            else:
                g = R.E("gpsimd", "dma_start", bw(SLW[0], DC * D), wcbf[:, :],
                        waits=reg_dep("l1w0") + [(sQ, sQ.n)], inc=sG, incn=16)
                wc_ready = (sG, g)
            tanh_t = bw(W_XTR, TC)
            for tc in range(NTC):
                ln_done = ln_tc(tc, lambda ic: bw(OFF_XN + ic * TC, TC))
                R.phase = "l1ffn"
                for oc in range(DC):
                    k = oc % 2
                    pj = pA(2 + k)
                    for ic in range(DC):
                        R.E("tensor", "matmul", pj, bw(SLW[0] + ic * D + oc * P, P),
                            bw(OFF_XN + ic * TC, TC),
                            start=(ic == 0), stop=(ic == DC - 1),
                            waits=[wc_ready, (sV, ln_done)] if ic == 0 else [],
                            inc=sT if ic == DC - 1 else None)
                    tj = sT.n
                    th = R.E("scalar", "activation", tanh_t, pj, AF.Tanh,
                             bias=scol("l1b", oc),
                             waits=[(sT, tj)] + reg_dep("tanht"), inc=sS)
                    ev = R.E("vector", "scalar_tensor_tensor",
                             xv(oc, tc * TC, TC), tanh_t, 0.5,
                             xv(oc, tc * TC, TC), OP.mult, OP.add,
                             waits=[(sS, th)], inc=sV)
                    reg_use("tanht", sV, ev)
            for i in range(3):
                reg_use(f"l1w{i}", sT, sT.n)

        def l1_attn():
            XN, QT, KT = W_R1, W_R2, W_R3
            ln_last = None
            for tc in range(NTC):
                ln_last = ln_tc(tc, lambda ic, tc=tc: bw(XN + ic * T + tc * TC, TC))
            spl = R.E("sync", "dma_start", xsp[:, :], SBf[:, 0:DC * T],
                      waits=[(sV, ln_last), (sS, sS.n)], inc=sQ, incn=16)
            g_wq = R.E("gpsimd", "dma_start", bw(XL_S0, DC * D),
                       ap["wq"].rearrange("(ic p) o -> p ic o", p=P),
                       waits=[(sQ, spl)], inc=sG, incn=16)
            g_wk = R.E("gpsimd", "dma_start", bw(XL_S1, DC * D),
                       ap["wk"].rearrange("(ic p) o -> p ic o", p=P), inc=sG, incn=16)
            R.E("gpsimd", "dma_start", bw(W_FEAT, 256, 0, DH), ap["wqf"][:, :],
                waits=reg_dep("feat"), inc=sG, incn=16)
            g_f1 = R.E("gpsimd", "dma_start", bw(W_FEAT, 256, 64, P), ap["wqf"][:, :],
                       inc=sG, incn=16)
            R.E("gpsimd", "dma_start", bw(W_FEAT + 256, 256, 0, DH), ap["wkf"][:, :],
                inc=sG, incn=16)
            g_f2 = R.E("gpsimd", "dma_start", bw(W_FEAT + 256, 256, 64, P), ap["wkf"][:, :],
                       inc=sG, incn=16)
            g_bv = R.E("gpsimd", "dma_start", bw(XL_BV, D, 0, 1), ap["bv_row"][:, :],
                       inc=sG, incn=16)
            vz = None
            for tt in range(16):
                onescol = bw(W_R4 + tt * 1040 + 64, 8 * 129).rearrange(
                    "p (h d) -> p h d", d=129)[:, :, 0:1]
                vz = R.E("vector", "memset", onescol, 1.0,
                         waits=reg_dep("vbufz") if tt == 0 else [], inc=sV)

            R.phase = "l1_qk"

            def proj_fm(dst, wslot, gdep, bname):
                last = None
                for oc in range(DC):
                    for tc in range(NTC):
                        k = tc % 2
                        pj = pA(k)
                        depp = reg_dep(f"qkps{k}")
                        for ic in range(DC):
                            R.E("tensor", "matmul", pj, bw(wslot + ic * D + oc * P, P),
                                bw(XN + ic * T + tc * TC, TC),
                                start=(ic == 0), stop=(ic == DC - 1),
                                waits=([(sG, gdep), (sV, ln_last)] + depp) if ic == 0 else [],
                                inc=sT if ic == DC - 1 else None)
                        last = R.E("scalar", "activation", bw(dst + oc * T + tc * TC, TC),
                                   pj, AF.Identity, bias=scol(bname, oc),
                                   waits=[(sT, sT.n)], inc=sS)
                        reg_use(f"qkps{k}", sS, last)
                return last

            q_done = proj_fm(QT, XL_S0, g_wq, "bq")
            g_wv = R.E("gpsimd", "dma_start", bw(XL_S0, DC * D),
                       ap["wv"].rearrange("(ic p) o -> p ic o", p=P),
                       waits=[(sT, sT.n), (sS, q_done)], inc=sG, incn=16)
            k_done = proj_fm(KT, XL_S1, g_wk, "bk")

            R.phase = "l1_v"
            for tt in range(16):
                for og in range(2):
                    pj = pB(og)
                    depp = reg_dep(f"vps{og}")
                    for ic in range(DC):
                        R.E("tensor", "matmul", pj, bw(XN + ic * T + tt * P, P),
                            bw(XL_S0 + ic * D + og * TC, TC),
                            start=(ic == 0), stop=False,
                            waits=([(sG, g_wv)] + depp) if ic == 0 else [],
                            inc=None)
                    R.E("tensor", "matmul", pj, bw(PO["allones"], P, 0, 1),
                        bw(XL_BV + og * TC, TC, 0, 1), start=False, stop=True,
                        waits=[(sG, g_bv)], inc=sT)
                    tj = sT.n
                    base = W_R4 + tt * 1040 + og * 4 * 129
                    pview = psB[:, og * 512: og * 512 + 512].rearrange(
                        "p (h e d) -> p h e d", h=4, e=2, d=64)
                    dve = bw(base, 4 * 129).rearrange("p (h d) -> p h d", d=129)[:, :, 0:64]
                    dvo = SBt[:, base + 65: base + 65 + 4 * 129].rearrange(
                        "p (h d) -> p h d", d=129)[:, :, 0:64]
                    R.E("vector", "tensor_copy", dve, pview[:, :, 0, :],
                        waits=[(sT, tj), (sV, vz)], inc=sV)
                    ev2 = R.E("vector", "tensor_copy", dvo, pview[:, :, 1, :],
                              waits=[(sT, tj)], inc=sV)
                    reg_use(f"vps{og}", sV, ev2)
            v_done = sV.n

            zc = R.E("vector", "memset", bw(XL_CTX, NH * 260), 0.0,
                     waits=[(sS, k_done)], inc=sV)

            R.phase = "l1_ctx"
            KP = XN
            for h in range(NH):
                fc, par = h // 2, h % 2
                p0 = par * 64
                hp = h // 2
                pend = None

                def emit_ctx(pd, par=par, hp=hp):
                    (nt, exv) = pd
                    blk = W_R4 + nt * 1040 + hp * 129
                    rhs = SBt[:, blk: blk + 65] if par == 0 else SBt[:, blk + 1: blk + 129]
                    nw = 65 if par == 0 else 128
                    for fh in range(2):
                        R.E("tensor", "matmul",
                            psB[:, 1024 + fh * 512: 1024 + fh * 512 + nw],
                            bw(KP + nt * 256 + fh * P, P), rhs,
                            start=(nt == 0), stop=(nt == 15),
                            waits=([(sS, exv)] +
                                   (reg_dep("ctxps") if nt == 0 and fh == 0 else [])),
                            inc=sT if nt == 15 else None)

                for nt in range(16):
                    pj = pA(nt % 2)
                    R.E("tensor", "matmul", pj[:, 0:256],
                        bw(KT + fc * T + nt * P, P, p0, p0 + 64),
                        bw(W_FEAT + 256, 256, p0, p0 + 64), start=True, stop=True,
                        waits=([(sG, g_f2), (sS, k_done), (sV, v_done)] +
                               (reg_dep("kpbuf") if nt == 0 else [])),
                        inc=sT)
                    tj = sT.n
                    if pend is not None:
                        emit_ctx(pend)
                    ex = R.E("scalar", "activation", bw(KP + nt * 256, 256), pj[:, 0:256],
                             AF.Exp, waits=[(sT, tj)], inc=sS)
                    pend = (nt, ex)
                emit_ctx(pend)
                t_ctx = sT.n
                reg_use("kpbuf", sT, t_ctx)
                cbase = XL_CTX + h * 260
                evs = None
                for fh in range(2):
                    if par == 0:
                        evs = R.E("vector", "tensor_copy", bw(cbase + fh * 130, 65),
                                  psB[:, 1024 + fh * 512: 1024 + fh * 512 + 65],
                                  waits=[(sT, t_ctx), (sV, zc)], inc=sV)
                    else:
                        evs = R.E("vector", "tensor_copy", bw(cbase + fh * 130 + 63, 65),
                                  psB[:, 1024 + fh * 512 + 63: 1024 + fh * 512 + 128],
                                  waits=[(sT, t_ctx), (sV, zc)], inc=sV)
                reg_use("ctxps", sV, evs)
            ctx_done = sV.n

            g_wp = R.E("gpsimd", "dma_start", bw(KT, DC * D),
                       ap["wproj"].rearrange("(ic p) o -> p ic o", p=P),
                       waits=[(sT, sT.n), (sV, ctx_done)], inc=sG, incn=16)

            R.phase = "l1_out"
            QP = KP + 4096
            for h in range(NH):
                fc, par = h // 2, h % 2
                p0 = par * 64
                mcol = 64 if par == 0 else 63
                cbase = XL_CTX + h * 260
                qp_ex = {}
                for fh in range(2):
                    for tc in range(NTC):
                        pj = pA(2 + (tc % 2))
                        R.E("tensor", "matmul", pj,
                            bw(W_FEAT + fh * P, P, p0, p0 + 64),
                            bw(QT + fc * T + tc * TC, TC, p0, p0 + 64),
                            start=True, stop=True,
                            waits=([(sG, g_f1)] +
                                   (reg_dep("qpbuf") if fh == 0 and tc == 0 else [])),
                            inc=sT)
                        qp_ex[(fh, tc)] = R.E("scalar", "activation",
                                              bw(QP + fh * T + tc * TC, TC),
                                              pj, AF.Exp, waits=[(sT, sT.n)], inc=sS)
                qp_done = sS.n
                pend2 = None

                def emit_norm(pd, par=par, fc=fc, p0=p0):
                    (nt4, nb, t_den, t_out) = pd
                    ad = R.E("scalar", "add", fv(XL_REC + nb * 1024, 512, 0, 1),
                             psA[0:1, (nt4 % 2) * 512:(nt4 % 2) * 512 + 512],
                             1e-6, waits=[(sT, t_den)], inc=sS)
                    rc = R.E("vector", "reciprocal", fv(XL_REC + nb * 1024, 512, 0, 1),
                             fv(XL_REC + nb * 1024, 512, 0, 1), waits=[(sS, ad)], inc=sV)
                    bc = R.E("tensor", "matmul", pA(nt4 % 2),
                             fv(PO["onesf"], P, 0, 1),
                             fv(XL_REC + nb * 1024, 512, 0, 1), start=True, stop=True,
                             waits=[(sV, rc)], inc=sT)
                    cb = R.E("scalar", "copy", fv(XL_RECB + (nt4 % 2) * 1024, 512),
                             pA(nt4 % 2),
                             waits=[(sT, bc)], inc=sS)
                    ao = R.E("vector", "tensor_mul",
                             bw(QT + fc * T + nt4 * TC, TC, p0, p0 + 64),
                             psB[p0:p0 + 64, nb * 512:(nb + 1) * 512],
                             fv(XL_RECB + (nt4 % 2) * 1024, 512, p0, p0 + 64),
                             waits=[(sS, cb)], inc=sV)
                    reg_use(f"l1avps{nb}", sV, ao)

                for nt4 in range(NTC):
                    nb = nt4 % 4
                    pj = psB[0:65, nb * 512:(nb + 1) * 512] if par == 0 else \
                        psB[0:128, nb * 512:(nb + 1) * 512]
                    depp = reg_dep(f"l1avps{nb}")
                    for fh in range(2):
                        lhs = (bw(cbase + fh * 130, 65) if par == 0
                               else bw(cbase + fh * 130, 128))
                        R.E("tensor", "matmul", pj, lhs, bw(QP + fh * T + nt4 * TC, TC),
                            start=(fh == 0), stop=(fh == 1),
                            waits=([(sS, qp_ex[(fh, nt4)])] + (depp if fh == 0 else [])),
                            inc=sT if fh == 1 else None)
                    t_out = sT.n
                    for fh in range(2):
                        kcol = (cbase + fh * 130 + 64) if par == 0 else \
                            (cbase + fh * 130 + 63)
                        R.E("tensor", "matmul",
                            psA[0:1, (nt4 % 2) * 512:(nt4 % 2) * 512 + 512],
                            bw(kcol, 1), bw(QP + fh * T + nt4 * TC, TC),
                            start=(fh == 0), stop=(fh == 1),
                            waits=[], inc=sT if fh == 1 else None)
                    t_den = sT.n
                    pend2 = (nt4, nb, t_den, t_out)
                    emit_norm(pend2)
                reg_use("qpbuf", sV, sV.n)
            aout_done = sV.n

            R.phase = "l1_proj"
            xr = R.E("sync", "dma_start", SBf[:, 0:DC * T], xsp[:, :],
                     waits=[(sV, aout_done), (sS, sS.n), (sT, sT.n)], inc=sQ, incn=16)

            for tc in range(NTC):
                for oc in range(DC):
                    k = (tc * DC + oc) % 2
                    pj = pA(2 + k)
                    depp = reg_dep(f"pjps{k}")
                    for ic in range(DC):
                        R.E("tensor", "matmul", pj, bw(KT + ic * D + oc * P, P),
                            bw(QT + ic * T + tc * TC, TC),
                            start=(ic == 0), stop=(ic == DC - 1),
                            waits=([(sG, g_wp), (sV, aout_done), (sQ, xr)] + depp)
                            if ic == 0 else [], inc=sT if ic == DC - 1 else None)
                    tj = sT.n
                    ev = R.E("vector", "scalar_tensor_tensor", xv(oc, tc * TC, TC),
                             pj, scol("bproj", oc), xv(oc, tc * TC, TC), OP.add, OP.add,
                             waits=[(sT, tj)], inc=sV)
                    reg_use(f"pjps{k}", sV, ev)

        if steps >= 1:
            ffn0_app()
            barrier()
        if steps >= 2:
            t_r4 = l0_attn()
            barrier()
            for s_ in range(4):
                lastuse[f"fs0_{s_}"] = [(sT, t_r4)]
                lastuse[f"fs1_{s_}"] = [(sT, t_r4)]
                lastuse[f"fh1_{s_}"] = [(sT, t_r4)]
        if steps >= 3:
            ffn0_app()
            barrier()
        if steps >= 4:
            l1_ffn(True)
            barrier()
        if steps >= 5:
            l1_attn()
            barrier()
        if steps >= 6:
            l1_ffn(False)
            barrier()

        R.phase = "xstore"
        ev_last = {}
        for tt in range(16):
            sl = tt % 2
            last_ev = None
            for ic in range(DC):
                bk = (tt * DC + ic) % 4
                dep = reg_dep(f"ostg{sl}") if ic == 0 else []
                tv = R.E("tensor", "transpose", pA(bk)[:, 0:P],
                         xv(ic, tt * P, P), fv(PO["idf"], P),
                         waits=dep + ([(sV, ev_last[bk])] if bk in ev_last else []),
                         inc=sT)
                last_ev = R.E("vector", "tensor_copy", stg[sl][:, ic * P:(ic + 1) * P],
                              pA(bk)[:, 0:P], waits=[(sT, tv)], inc=sV)
                ev_last[bk] = last_ev
            q = R.E("sync", "dma_start", out[tt * P:(tt + 1) * P, :], stg[sl],
                    waits=[(sV, last_ev)], inc=sQ, incn=16)
            reg_use(f"ostg{sl}", sQ, q)
        R.wait("sync", (sQ, sQ.n))

        @block.tensor
        def _(h):
            for f in R.ops["tensor"]:
                f(h)

        @block.vector
        def _(h):
            for f in R.ops["vector"]:
                f(h)

        @block.scalar
        def _(h):
            for f in R.ops["scalar"]:
                f(h)

        @block.sync
        def _(h):
            for f in R.ops["sync"]:
                f(h)

        @block.gpsimd
        def _(h):
            for f in R.ops["gpsimd"]:
                f(h)

    nc._phase_tags = list(R.ttags)
    return nc


def _host_consts():
    t = np.arange(T, dtype=np.float64)
    inv = 1.0 / (10000.0 ** (np.arange(0, DH, 2, dtype=np.float64) / DH))
    fr = t[:, None] * inv[None, :]
    emb = np.concatenate([fr, fr], axis=1)
    cos = np.cos(emb).astype(np.float32).T
    sin = np.sin(emb).astype(np.float32).T
    cosb = np.concatenate([cos, cos], axis=0)
    sinb = np.concatenate([sin, sin], axis=0)
    blk = np.zeros((DH, DH), np.float32)
    for i in range(32):
        blk[i + 32, i] = -1.0
        blk[i, i + 32] = 1.0
    prot = np.zeros((P, P), np.float32)
    prot[:DH, :DH] = blk
    prot[DH:, DH:] = blk
    idf = np.eye(P, dtype=np.float32)
    return cosb, sinb, prot, idf


def _col128(v):
    v = np.asarray(v, dtype=np.float32).reshape(-1, P)
    return np.ascontiguousarray(v.T)


def kernel(**kw):
    cosb, sinb, prot, idf = _host_consts()
    base = {
        "w1": kw["L0_ffn_w1_w"], "w2": kw["L0_ffn_w2_w"],
        "h1": kw["L0_ffn_h1_w"], "h2": kw["L0_ffn_h2_w"], "w3": kw["L0_ffn_w3_w"],
        "s1c": _col128(kw["L0_ffn_w1_s"]), "s2c": _col128(kw["L0_ffn_w2_s"]),
        "sh1c": _col128(kw["L0_ffn_h1_s"]), "sh2c": _col128(kw["L0_ffn_h2_s"]),
        "sw3c": _col128(kw["L0_ffn_w3_s"]),
        "wdk": kw["L0_attn_w_dk"], "wdv": kw["L0_attn_w_dv"],
        "wuk": kw["L0_attn_w_uk"], "wuv": kw["L0_attn_w_uv"], "wo": kw["L0_attn_w_o"],
        "bdkc": _col128(kw["L0_attn_b_dk"]), "bdvc": _col128(kw["L0_attn_b_dv"]),
        "bukc": _col128(kw["L0_attn_b_uk"]),
        "buv_row": np.asarray(kw["L0_attn_b_uv"], np.float32).reshape(1, D),
        "boc": _col128(kw["L0_attn_b_o"]),
        "cosb": cosb, "sinb": sinb, "prot": prot, "idf": idf,
        "wq": kw["L1_attn_Wq"], "wk": kw["L1_attn_Wk"], "wv": kw["L1_attn_Wv"],
        "wproj": kw["L1_attn_Wproj"],
        "bqc": _col128(kw["L1_attn_bq"]), "bkc": _col128(kw["L1_attn_bk"]),
        "bv_row": np.asarray(kw["L1_attn_bv"], np.float32).reshape(1, D),
        "bprojc": _col128(kw["L1_attn_bproj"]),
        "wqf": kw["L1_attn_wq_feat"], "wkf": kw["L1_attn_wk_feat"],
        "l1w": kw["L1_ffn_weight"], "l1a1": kw["L1_ffn_aux_weight1"],
        "l1a2": kw["L1_ffn_aux_weight2"], "l1bc": _col128(kw["L1_ffn_bias"]),
    }
    base = {k: np.ascontiguousarray(np.asarray(v, dtype=np.float32)) for k, v in base.items()}
    x = np.asarray(kw["L0_attn_x"], dtype=np.float32)
    nc = build_nc(STEPS)
    in_maps = []
    for b in range(8):
        m = dict(base)
        m["x"] = np.ascontiguousarray(x[b])
        in_maps.append(m)
    res = run_bass_kernel_spmd(nc, in_maps, core_ids=list(range(8)))
    return np.stack([np.asarray(res.results[b]["out"]) for b in range(8)], axis=0)
